# version 11
# baseline (speedup 1.0000x reference)
import os
import numpy as np
import ml_dtypes
import concourse.bass as bass
import concourse.mybir as mybir
from concourse.bass_utils import run_bass_kernel_spmd
from contextlib import ExitStack

F32 = mybir.dt.float32
BF16 = mybir.dt.bfloat16
AF = mybir.ActivationFunctionType
ALU = mybir.AluOpType
AX = mybir.AxisListType
bf16 = ml_dtypes.bfloat16

NCORE = 8
D = 1024
NE = 8
TPC = 8192
TILE = 512
NTILE = TPC // TILE
NTT = TILE // 128
PI = float(np.pi)


def _split_hl(a):
    hi = a.astype(bf16)
    lo = (a.astype(np.float32) - hi.astype(np.float32)).astype(bf16)
    return np.ascontiguousarray(hi), np.ascontiguousarray(lo)


class Sched:
    ENGS = ("pe", "dve", "act", "sync", "gp")

    def __init__(self):
        self.ops = {k: [] for k in self.ENGS}
        self.val = {}
        self.lastw = {}

    def sem(self, name):
        self.val[name] = 0

    def do(self, eng, fn, waits=(), inc=None, inc_n=1):
        wl = []
        for sname, v in waits:
            if v is None or v <= 0:
                continue
            key = (eng, sname)
            if self.lastw.get(key, 0) >= v:
                continue
            self.lastw[key] = v
            wl.append((sname, v))
        rec = None
        if inc is not None:
            self.val[inc] += inc_n
            rec = (inc, inc_n)
        self.ops[eng].append((wl, fn, rec))
        return self.val[inc] if inc is not None else None


_DBG = os.environ.get("KDBG", "") == "1"
_SER = os.environ.get("KSER", "")


def build_nc():
    nc = bass.Bass("TRN2", target_bir_lowering=False)
    dbg = {}
    if _DBG:
        for nm, shape, dt_ in (
                ("t_crevh", [128, 8192], BF16), ("t_crevl", [128, 8192], BF16),
                ("t_kneg", [1, 1024], F32), ("t_d", [128, 1], F32),
                ("t_fu", [128, 8], F32), ("t_fv", [128, 8], F32),
                ("t_ehi", [128, 192], BF16), ("t_elo", [128, 192], BF16),
                ("t_wrevh", [128, 192], BF16),
                ("t_xa_attn", [128, 4096], F32),
                ("t_xlnt2", [128, 4096], BF16),
                ("t_g", [128, 32], F32),
                ("t_ht", [128, 16384], BF16),
                ("t_bits", [24, 128], BF16), ("t_corr", [24, 128], BF16),
                ("t_w1b", [128, 4096], BF16),
                ("t_ahi", [128, 1024], BF16),
                ("t_fxa1", [128, 4096], F32), ("t_fxa2", [128, 4096], F32),
                ("t_fht", [128, 16384], BF16), ("t_fxa3", [128, 4096], F32),
                ("t_s0", [1, 1024], F32), ("t_cnt", [1, 16], F32),
                ("t_stat", [128, 16], F32), ("t_x0", [1, 1024], F32),
                ("t_junk", [128, 1024], BF16), ("t_temp1", [128, 1024], F32)):
            dbg[nm] = nc.dram_tensor(nm, shape, dt_, kind="ExternalOutput").ap()

    xs = nc.dram_tensor("xs", [TPC, D], F32, kind="ExternalInput").ap()
    x0row = nc.dram_tensor("x0row", [1, D], F32, kind="ExternalInput").ap()
    w1r = nc.dram_tensor("w1r", [NE, 8, 128, 8, 512], F32, kind="ExternalInput").ap()
    w2r = nc.dram_tensor("w2r", [NE, 32, 128, D], F32, kind="ExternalInput").ap()
    welr = nc.dram_tensor("welr", [D, 24], F32, kind="ExternalInput").ap()
    belr = nc.dram_tensor("belr", [24, 1], F32, kind="ExternalInput").ap()
    wler = nc.dram_tensor("wler", [24, D], F32, kind="ExternalInput").ap()
    bler = nc.dram_tensor("bler", [1, D], F32, kind="ExternalInput").ap()
    wgr = nc.dram_tensor("wgr", [D, NE], F32, kind="ExternalInput").ap()
    bgr = nc.dram_tensor("bgr", [1, NE], F32, kind="ExternalInput").ap()
    b1r = nc.dram_tensor("b1r", [NE, 128, 32], F32, kind="ExternalInput").ap()
    b2r = nc.dram_tensor("b2r", [NE, D], F32, kind="ExternalInput").ap()
    ccos_hi = nc.dram_tensor("ccos_hi", [D, D], BF16, kind="ExternalInput").ap()
    ccos_lo = nc.dram_tensor("ccos_lo", [D, D], BF16, kind="ExternalInput").ap()
    csin_hi = nc.dram_tensor("csin_hi", [D, D], BF16, kind="ExternalInput").ap()
    csin_lo = nc.dram_tensor("csin_lo", [D, D], BF16, kind="ExternalInput").ap()
    ckap = nc.dram_tensor("ckap", [128, 8], F32, kind="ExternalInput").ap()
    ckap2 = nc.dram_tensor("ckap2", [128, 8], F32, kind="ExternalInput").ap()
    cwc2 = nc.dram_tensor("cwc2", [1, 5], F32, kind="ExternalInput").ap()
    cident = nc.dram_tensor("cident", [128, 128], BF16, kind="ExternalInput").ap()
    cht = nc.dram_tensor("cht", [24, 12], BF16, kind="ExternalInput").ap()
    cones = nc.dram_tensor("cones", [1, 128], BF16, kind="ExternalInput").ap()
    cjrev = nc.dram_tensor("cjrev", [128, 128], BF16, kind="ExternalInput").ap()
    out = nc.dram_tensor("out", [TPC, D], F32, kind="ExternalOutput").ap()
    w1bf = nc.dram_tensor("w1bf", [NE, 8, 128, 4096], BF16, kind="Internal").ap()
    w2bf = nc.dram_tensor("w2bf", [NE, 2, 8, 128, 2048], BF16, kind="Internal").ap()
    k2r_hi = nc.dram_tensor("k2r_hi", [2048], BF16, kind="Internal").ap()
    k2r_lo = nc.dram_tensor("k2r_lo", [2048], BF16, kind="Internal").ap()
    dbr = nc.dram_tensor("dbr", [128], F32, kind="Internal").ap()

    s = Sched()
    SEMNAMES = ("c", "cw", "db", "st", "gp", "pe", "dve", "act",
                "x0", "x1s",
                "w1a", "w1b", "w1c", "w2a", "w2b", "w2c", "w2d",
                "cla", "clb", "csa", "csb")
    for nm in SEMNAMES:
        s.sem(nm)

    with ExitStack() as es:
        def sbt(name, shape, dtype):
            return es.enter_context(nc.sbuf_tensor(name, shape, dtype))

        def psa(name, shape, dtype):
            return es.enter_context(nc.psum_tensor(name, shape, dtype))

        sems = {nm: es.enter_context(nc.semaphore(nm)) for nm in SEMNAMES}

        xa = [sbt("xa0", [128, NTT * D], F32), sbt("xa1", [128, NTT * D], F32)]
        xlnT2 = sbt("xlnT2", [128, 8 * TILE], BF16)
        a_hi = [sbt("a_hi0", [128, 8 * 128], BF16), sbt("a_hi1", [128, 8 * 128], BF16)]
        a_lo = [sbt("a_lo0", [128, 8 * 128], BF16), sbt("a_lo1", [128, 8 * 128], BF16)]
        l2lo = None
        hT = sbt("hT", [128, 32 * TILE], BF16)
        crev_hi = sbt("crev_hi", [128, 8 * D], BF16)
        crev_lo = sbt("crev_lo", [128, 8 * D], BF16)
        W1R, W2R = 3, 4
        w1buf = [sbt(f"w1b{i}", [128, 4096], BF16) for i in range(W1R)]
        w2buf = [sbt(f"w2b{i}", [128, 2048], BF16) for i in range(W2R)]
        xst_hi = sbt("xsth0", [128, D], BF16)
        xst_lo = sbt("xstl0", [128, D], BF16)
        temp1 = sbt("temp10", [128, D], F32)
        junkbf = temp1[32:33, :]
        stat = sbt("stat", [128, 16], F32)
        G = sbt("G", [128, NTT * NE], F32)
        gbf = sbt("gbf", [128, NE], BF16)
        gt_sb = sbt("gt_sb", [8, TILE], BF16)
        e_hi = sbt("e_hi", [128, 8 * 24], BF16)
        e_lo = sbt("e_lo", [128, 8 * 24], BF16)
        wlerev_hi = sbt("wlerev_hi", [24, D], BF16)
        wlerev_lo = sbt("wlerev_lo", [24, D], BF16)
        blerev_hi = sbt("blerev_hi", [1, D], BF16)
        blerev_lo = sbt("blerev_lo", [1, D], BF16)
        bel_sb = sbt("bel_sb", [24, 1], F32)
        wg_hi = sbt("wg_hi", [128, 8 * NE], BF16)
        wg_lo = sbt("wg_lo", [128, 8 * NE], BF16)
        bg_hi = sbt("bg_hi", [1, NE], BF16)
        bg_lo = sbt("bg_lo", [1, NE], BF16)
        b1sb = sbt("b1sb", [128, NE * 32], F32)
        b2_hi = sbt("b2_hi", [8, D], BF16)
        b2_lo = sbt("b2_lo", [8, D], BF16)
        ident = sbt("ident", [128, 128], BF16)
        jrev = sbt("jrev", [128, 128], BF16)
        htc = sbt("htc", [24, 12], BF16)
        ones = sbt("ones", [1, 128], BF16)
        bitsT = [sbt("bitsT0", [24, 128], BF16), sbt("bitsT1", [24, 128], BF16)]
        syn01 = [sbt("syn010", [128, 12], BF16), sbt("syn011", [128, 12], BF16)]
        synw = sbt("synw", [128, 8], F32)
        syneff = [sbt("syneff0", [128, 12], BF16), sbt("syneff1", [128, 12], BF16)]
        synT = [sbt("synT0", [12, 128], BF16), sbt("synT1", [12, 128], BF16)]
        corrT = [sbt("corrT0", [24, 128], BF16), sbt("corrT1", [24, 128], BF16)]
        logit_sb = sbt("logit_sb", [128, NE], F32)
        gtmp = sbt("gtmp", [128, 4 * NE], F32)
        gstat = sbt("gstat", [128, 8], F32)

        l2lo = a_lo
        pb = [psa(f"pb{i}", [128, 512], F32) for i in range(6)]
        pt = psa("pt", [128, 1024], BF16)
        pl = psa("pl", [128, 512], F32)

        x0sb = sbt("x0sb", [1, D], F32)
        s0sb = x0sb
        cnt5 = sbt("cnt5", [1, 16], F32)
        wc2sb = sbt("wc2sb", [1, 8], F32)
        dstat = sbt("dstat", [1, 8], F32)
        Dsb = sbt("Dsb", [1, 1], F32)
        D128 = sbt("D128", [128, 1], F32)
        dcoef = sbt("dcoef", [128, 8], F32)
        kap = sbt("kap", [128, 8], F32)
        kap2 = sbt("kap2", [128, 8], F32)
        farg = sbt("farg", [128, 8], F32)
        fphase = sbt("fphase", [128, 8], F32)
        freal = sbt("freal", [128, 8], F32)
        fcos = sbt("fcos", [128, 8], F32)
        fsin = sbt("fsin", [128, 8], F32)
        fu = sbt("fu", [128, 8], F32)
        fv = sbt("fv", [128, 8], F32)
        u_hi = sbt("u_hi", [128, 8], BF16)
        u_lo = sbt("u_lo", [128, 8], BF16)
        v_hi = sbt("v_hi", [128, 8], BF16)
        v_lo = sbt("v_lo", [128, 8], BF16)
        cosbuf = w2buf[0]
        knegsb = x0sb
        kneg_hi = hT[0:1, 4 * D:5 * D]
        kneg_lo = hT[0:1, 5 * D:6 * D]
        welsb = sbt("welsb", [128, 8 * 24], F32)
        wel_hi = sbt("wel_hi", [128, 8 * 24], BF16)
        wel_lo = sbt("wel_lo", [128, 8 * 24], BF16)
        welrev_hi = sbt("welrev_hi", [128, 8 * 24], BF16)
        welrev_lo = sbt("welrev_lo", [128, 8 * 24], BF16)
        wlesb = temp1[0:24, :]
        blesb = temp1[0:1, :]
        wgsb = sbt("wgsb", [128, 8 * NE], F32)
        bgsb = sbt("bgsb", [1, NE], F32)
        b2sb = sbt("b2sb", [8, D], F32)
        tpbuf_hi = sbt("tpbuf_hi", [128, 128], BF16)
        tpbuf_lo = sbt("tpbuf_lo", [128, 128], BF16)
        stage32 = [sbt("stg320", [128, 1024], F32), sbt("stg321", [128, 1024], F32)]
        stagebf = [sbt("stgbf0", [128, 1024], BF16), sbt("stgbf1", [128, 1024], BF16)]

        def dve(fn, waits=()):
            return s.do("dve", fn, waits=waits, inc="dve")

        def dves(fn, waits=()):
            return s.do("dve", fn,
                        waits=list(waits) + [("dve", s.val["dve"])], inc="dve")

        def act(fn, waits=()):
            return s.do("act", fn, waits=waits, inc="act")

        def acts(fn, waits=()):
            return s.do("act", fn,
                        waits=list(waits) + [("act", s.val["act"])], inc="act")

        def pe(fn, waits=(), inc=None):
            return s.do("pe", fn, waits=waits, inc=inc)

        def cload(dst, src):
            return s.do("sync", lambda e, d=dst, sr=src: e.dma_start(d, sr),
                        inc="c", inc_n=16)

        cload(ident[:, :], cident)
        cload(jrev[:, :], cjrev)
        cload(htc[:, :], cht)
        cload(ones[:, :], cones)
        cload(kap[:, :], ckap)
        cload(kap2[:, :], ckap2)
        cload(wc2sb[:, :5], cwc2)
        ms_x0 = cload(x0sb[:, :], x0row)
        for dc in range(8):
            cload(welsb[:, dc * 24:(dc + 1) * 24], welr[dc * 128:(dc + 1) * 128, :])
        cload(wlesb[:, :], wler)
        cload(bel_sb[:, :], belr)
        for dc in range(8):
            cload(wgsb[:, dc * NE:(dc + 1) * NE], wgr[dc * 128:(dc + 1) * 128, :])
        cload(bgsb[:, :], bgr)
        for ee in range(NE):
            cload(b1sb[:, ee * 32:(ee + 1) * 32], b1r[ee])
        ms_c_all = cload(b2sb[:, :], b2r)

        def emit_xload(t):
            ap_in = xs[t * TILE:(t + 1) * TILE, :].rearrange(
                "(a p) d -> p a d", p=128)
            dst = xa[t % 2][:, :].rearrange("p (a d) -> p a d", a=NTT)
            return lambda e, d=dst, sr=ap_in: e.dma_start(d, sr)

        xS = ("x0", "x1s")
        s.do("gp", emit_xload(0), inc=xS[0], inc_n=16)
        s.do("gp", emit_xload(1), inc=xS[1], inc_n=16)

        ms_conv = [0] * NE
        conv_chunks = []
        conv_bounds = []
        for ee in range(NE):
            for jg in range(8):
                w1flat = w1r[ee, jg].rearrange("p a b -> p (a b)")
                for k4 in range(4):
                    conv_chunks.append((w1flat[:, k4 * D:(k4 + 1) * D],
                                        w1bf[ee, jg][:, k4 * D:(k4 + 1) * D]))
            for jc in range(32):
                conv_chunks.append((w2r[ee, jc], [
                    w2bf[ee, db, jc // 4, :, (jc % 4) * 512:(jc % 4) * 512 + 512]
                    for db in range(2)]))
            conv_bounds.append(len(conv_chunks))
        nch = len(conv_chunks)
        clS = ("cla", "clb")
        csS = ("csa", "csb")

        def emit_conv_range(q0, q1):
            for q in range(q0, q1 + 1):
                if q < q1:
                    wv = 0
                    if q >= 2:
                        wv = sum((32 if isinstance(conv_chunks[i][1], list)
                                  else 16)
                                 for i in range(q % 2, q - 1, 2))
                    s.do("sync", lambda e, sl=q % 2, src=conv_chunks[q][0]:
                         e.dma_start(stage32[sl][:, :], src),
                         waits=[(csS[q % 2], wv)],
                         inc=clS[q % 2], inc_n=16)
                if q >= 1 and (q - 1) >= q0:
                    cp = s.do("gp", lambda e, sl=(q - 1) % 2:
                              e.tensor_copy(stagebf[sl][:, :], stage32[sl][:, :]),
                              waits=[(clS[(q - 1) % 2], 16 * ((q - 1) // 2 + 1))],
                              inc="gp")
                    dsts = conv_chunks[q - 1][1]
                    if not isinstance(dsts, list):
                        s.do("gp", lambda e, sl=(q - 1) % 2, dst=dsts:
                             e.dma_start(dst, stagebf[sl][:, :]),
                             waits=[("gp", cp)], inc=csS[(q - 1) % 2], inc_n=16)
                    else:
                        s.do("gp", lambda e, sl=(q - 1) % 2, dst=dsts[0]:
                             e.dma_start(dst, stagebf[sl][:, 0:512]),
                             waits=[("gp", cp)], inc=csS[(q - 1) % 2], inc_n=16)
                        s.do("gp", lambda e, sl=(q - 1) % 2, dst=dsts[1]:
                             e.dma_start(dst, stagebf[sl][:, 512:1024]),
                             inc=csS[(q - 1) % 2], inc_n=16)

        def _cs_counts(nb):
            c = [0, 0]
            for q in range(nb):
                kind = conv_chunks[q][1]
                c[q % 2] += 32 if isinstance(kind, list) else 16
            return c
        ms_conv = []
        for ee in range(NE):
            c0, c1 = _cs_counts(conv_bounds[ee])
            ms_conv.append(((csS[0], c0), (csS[1], c1)))

        dves(lambda e: e.reduce_sum(dstat[:, 0:1], x0sb[:, :], axis=AX.X),
            waits=[("c", ms_c_all)])
        dves(lambda e: e.tensor_scalar_mul(dstat[:, 0:1], dstat[:, 0:1], 1.0 / D))
        dves(lambda e: e.tensor_scalar_sub(s0sb[:, :], x0sb[:, :], dstat[:, 0:1]))
        dves(lambda e: e.tensor_mul(junkbf[:, :], s0sb[:, :], s0sb[:, :]))
        dves(lambda e: e.reduce_sum(dstat[:, 1:2], junkbf[:, :], axis=AX.X))
        ms_v0 = dves(lambda e: e.tensor_scalar(
            dstat[:, 2:3], dstat[:, 1:2], 1.0 / D, 1e-5,
            op0=ALU.mult, op1=ALU.add))
        acts(lambda e: e.activation(dstat[:, 3:4], dstat[:, 2:3], AF.Ln),
            waits=[("dve", ms_v0)])
        ms_r0 = acts(lambda e: e.activation(dstat[:, 4:5], dstat[:, 3:4],
                                           AF.Exp, scale=-0.5))
        dves(lambda e: e.tensor_scalar_mul(s0sb[:, :], s0sb[:, :], dstat[:, 4:5]),
            waits=[("act", ms_r0)])
        for i, n in enumerate((4, 16, 64, 256, 1024)):
            dves(lambda e, n=n: e.reduce_sum(dstat[:, 5:6], s0sb[:, :n], axis=AX.X))
            dves(lambda e, n=n: e.tensor_scalar_mul(dstat[:, 5:6],
                                                   dstat[:, 5:6], 1.0 / n))
            dves(lambda e, n=n: e.tensor_single_scalar(
                junkbf[:, :n], s0sb[:, :n], dstat[:1, 5:6], op=ALU.is_gt))
            dves(lambda e, n=n, i=i: e.reduce_sum(
                cnt5[:, i:i + 1], junkbf[:, :n], axis=AX.X))
        ms_cnt = dves(lambda e: e.tensor_scalar_max(cnt5[:, :5], cnt5[:, :5], 1.0))
        ms_ly = acts(lambda e: e.activation(cnt5[:, 8:13], cnt5[:, :5], AF.Ln),
                    waits=[("dve", ms_cnt)])
        dves(lambda e: e.tensor_mul(cnt5[:, 8:13], cnt5[:, 8:13], wc2sb[:, :5]),
            waits=[("act", ms_ly)])
        dves(lambda e: e.reduce_sum(Dsb[:, :], cnt5[:, 8:13], axis=AX.X))
        dves(lambda e: e.tensor_scalar_max(Dsb[:, :], Dsb[:, :], 1.0))
        ms_D = dves(lambda e: e.tensor_scalar_min(Dsb[:, :], Dsb[:, :], 2.0))

        dves(lambda e: e.tensor_copy(u_hi[0:1, 0:1], Dsb[:, :]),
            waits=[("dve", ms_D) if False else ("c", ms_c_all)])
        ms_dsp = dves(lambda e: e.scalar_tensor_tensor(
            u_lo[0:1, 0:1], Dsb[:, :], 1.0, u_hi[0:1, 0:1],
            op0=ALU.mult, op1=ALU.subtract))
        mdb1 = pe(lambda e: e.matmul(pb[3][:, 0:1], ones[:, :],
                                     u_hi[0:1, 0:1], start=True, stop=False),
                  waits=[("dve", ms_dsp), ("c", ms_c_all)], inc=None)
        mdb2 = pe(lambda e: e.matmul(pb[3][:, 0:1], ones[:, :],
                                     u_lo[0:1, 0:1], start=False, stop=True),
                  inc="pe")
        ms_d128 = dves(lambda e: e.tensor_copy(D128[:, :], pb[3][:, 0:1]),
                      waits=[("pe", mdb2)])

        dves(lambda e: e.tensor_scalar(dcoef[:, 0:1], D128[:, :], D128[:, 0:1],
                                      None, op0=ALU.mult),
            waits=[("c", ms_c_all)])
        dves(lambda e: e.tensor_scalar(dcoef[:, 1:2], dcoef[:, 0:1], 0.2,
                                      D128[:, 0:1], op0=ALU.mult, op1=ALU.add))
        dves(lambda e: e.tensor_scalar(dcoef[:, 2:3], D128[:, :], 0.5, 1.0,
                                      op0=ALU.mult, op1=ALU.add))
        dves(lambda e: e.tensor_scalar_mul(dcoef[:, 3:4], D128[:, :], 0.1))
        dves(lambda e: e.tensor_scalar(farg[:, :], kap[:, :], dcoef[:, 2:3],
                                      dcoef[:, 1:2], op0=ALU.mult, op1=ALU.add))
        dves(lambda e: e.tensor_scalar(fphase[:, :], kap2[:, :], dcoef[:, 3:4],
                                      dcoef[:, 1:2], op0=ALU.mult, op1=ALU.add))
        dves(lambda e: e.tensor_sub(fphase[:, :], fphase[:, :], kap[:, :]))
        dves(lambda e: e.tensor_scalar(fcos[:, :], farg[:, :], PI, None,
                                      op0=ALU.is_gt))
        ms_arg = dves(lambda e: e.scalar_tensor_tensor(
            farg[:, :], fcos[:, :], -2.0 * PI, farg[:, :],
            op0=ALU.mult, op1=ALU.add))
        ms_arg2 = dves(lambda e: e.tensor_scalar(
            fu[:, :], fphase[:, :], -1.0, PI / 2, op0=ALU.mult, op1=ALU.add))
        acts(lambda e: e.activation(freal[:, :], farg[:, :], AF.Sin),
            waits=[("dve", ms_arg2)])
        acts(lambda e: e.activation(fcos[:, :], fu[:, :], AF.Sin))
        ms_sin = acts(lambda e: e.activation(fsin[:, :], fphase[:, :], AF.Sin))
        dves(lambda e: e.tensor_mul(fu[:, :], freal[:, :], fcos[:, :]),
            waits=[("act", ms_sin)])
        dves(lambda e: e.tensor_mul(fv[:, :], freal[:, :], fsin[:, :]))
        dves(lambda e: e.tensor_copy(u_hi[:, :], fu[:, :]))
        dves(lambda e: e.scalar_tensor_tensor(u_lo[:, :], fu[:, :], 1.0,
                                             u_hi[:, :], op0=ALU.mult,
                                             op1=ALU.subtract))
        dves(lambda e: e.tensor_copy(v_hi[:, :], fv[:, :]))
        ms_uv = dves(lambda e: e.scalar_tensor_tensor(
            v_lo[:, :], fv[:, :], 1.0, v_hi[:, :],
            op0=ALU.mult, op1=ALU.subtract))

        ms_kneg_mm = 0
        ncw = 0
        for jc in range(8):
            for half, (shi, slo, vh, vl) in enumerate(
                    ((ccos_hi, ccos_lo, u_hi, u_lo),
                     (csin_hi, csin_lo, v_hi, v_lo))):
                for q, src in enumerate((shi, slo)):
                    s.do("sync", lambda e, q=q, src=src, jc=jc:
                         e.dma_start(cosbuf[:, q * D:(q + 1) * D],
                                     src[jc * 128:(jc + 1) * 128, :]),
                         waits=[("pe", ms_kneg_mm)], inc="cw", inc_n=16)
                    ncw += 1
                ops3 = [(vh, 0), (vl, 0), (vh, 1)]
                for mb in range(2):
                    for oi, (usrc, q) in enumerate(ops3):
                        st_f = (jc == 0 and half == 0) and oi == 0
                        sp_f = (jc == 7 and half == 1) and oi == 2
                        ms_kneg_mm = pe(
                            lambda e, usrc=usrc, q=q, jc=jc, mb=mb,
                            st_f=st_f, sp_f=sp_f:
                            e.matmul(pb[mb][0:1, :], usrc[:, jc:jc + 1],
                                     cosbuf[:, q * D + mb * 512:q * D + (mb + 1) * 512],
                                     start=st_f, stop=sp_f),
                            waits=[("cw", 16 * ncw), ("dve", ms_uv)],
                            inc="pe" if (oi == 2 and mb == 1) else None)
        dves(lambda e: e.tensor_scalar_mul(knegsb[:, 0:512], pb[0][0:1, :], 1.0 / D),
            waits=[("pe", ms_kneg_mm)])
        dves(lambda e: e.tensor_scalar_mul(knegsb[:, 512:1024], pb[1][0:1, :], 1.0 / D))
        dves(lambda e: e.tensor_copy(kneg_hi[:, :], knegsb[:, :]))
        ms_ksp = dves(lambda e: e.scalar_tensor_tensor(
            kneg_lo[:, :], knegsb[:, :], 1.0, kneg_hi[:, :],
            op0=ALU.mult, op1=ALU.subtract))

        ms_k2 = 0
        for dst, hi in ((k2r_hi, kneg_hi), (k2r_lo, kneg_lo)):
            for off in (0, 1024):
                ms_k2 = s.do("sync", lambda e, dst=dst, hi=hi, off=off:
                             e.dma_start(dst[off:off + 1023], hi[0:1, 1:1024]),
                             waits=[("dve", ms_ksp)], inc="db", inc_n=16)
                ms_k2 = s.do("sync", lambda e, dst=dst, hi=hi, off=off:
                             e.dma_start(dst[off + 1023:off + 1024], hi[0:1, 0:1]),
                             inc="db", inc_n=16)
        ms_crev = ms_k2
        for ac in range(8):
            ms_crev = s.do("sync", lambda e, ac=ac: e.dma_start(
                crev_hi[:, ac * D:(ac + 1) * D],
                bass.AP(k2r_hi.tensor, ac * 128, [[1, 128], [1, D]])),
                waits=[("db", ms_k2)], inc="db", inc_n=16)
            ms_crev = s.do("sync", lambda e, ac=ac: e.dma_start(
                crev_lo[:, ac * D:(ac + 1) * D],
                bass.AP(k2r_lo.tensor, ac * 128, [[1, 128], [1, D]])),
                inc="db", inc_n=16)

        dves(lambda e: e.tensor_copy(wel_hi[:, :], welsb[:, :]),
            waits=[("c", ms_c_all)])
        ms_welsp = dves(lambda e: e.scalar_tensor_tensor(
            wel_lo[:, :], welsb[:, :], 1.0, wel_hi[:, :],
            op0=ALU.mult, op1=ALU.subtract))
        ms_wrev = 0
        for bc in range(8):
            src_c = 7 - bc
            m1 = pe(lambda e, src_c=src_c: e.matmul(
                pl[:, 0:24], jrev[:, :], wel_hi[:, src_c * 24:(src_c + 1) * 24],
                start=True, stop=True),
                waits=[("dve", ms_welsp), ("c", ms_c_all)], inc="pe")
            d1 = dves(lambda e, bc=bc: e.tensor_copy(
                welrev_hi[:, bc * 24:(bc + 1) * 24], pl[:, 0:24]),
                waits=[("pe", m1)])
            m2 = pe(lambda e, src_c=src_c: e.matmul(
                pl[:, 24:48], jrev[:, :], wel_lo[:, src_c * 24:(src_c + 1) * 24],
                start=True, stop=True), waits=[("dve", d1)], inc="pe")
            ms_wrev = dves(lambda e, bc=bc: e.tensor_copy(
                welrev_lo[:, bc * 24:(bc + 1) * 24], pl[:, 24:48]),
                waits=[("pe", m2)])

        ms_E = 0
        for ac in range(8):
            n_mm = 0
            ms_emm = 0
            for bc in range(8):
                mt1 = pe(lambda e, ac=ac, bc=bc: e.transpose(
                    pt[:, 0:128],
                    crev_hi[:, ac * D + bc * 128:ac * D + (bc + 1) * 128],
                    ident[:, :]),
                    waits=[("db", ms_crev), ("dve", ms_wrev)], inc="pe")
                e1 = acts(lambda e: e.copy(tpbuf_hi[:, :], pt[:, 0:128]),
                         waits=[("pe", mt1)])
                mt2 = pe(lambda e, ac=ac, bc=bc: e.transpose(
                    pt[:, 128:256],
                    crev_lo[:, ac * D + bc * 128:ac * D + (bc + 1) * 128],
                    ident[:, :]), waits=[("act", e1)], inc="pe")
                e2 = acts(lambda e: e.copy(tpbuf_lo[:, :], pt[:, 128:256]),
                         waits=[("pe", mt2)])
                for tb, wv in ((tpbuf_hi, welrev_hi), (tpbuf_lo, welrev_hi),
                               (tpbuf_hi, welrev_lo)):
                    ms_emm = pe(lambda e, tb=tb, wv=wv, bc=bc,
                                st_f=(n_mm == 0), sp_f=(n_mm == 23):
                                e.matmul(pb[2][:, 0:24], tb[:, :],
                                         wv[:, bc * 24:(bc + 1) * 24],
                                         start=st_f, stop=sp_f),
                                waits=[("act", e2)],
                                inc="pe" if n_mm == 23 else None)
                    n_mm += 1
            dves(lambda e, ac=ac: e.tensor_copy(
                e_hi[:, ac * 24:(ac + 1) * 24], pb[2][:, 0:24]),
                waits=[("pe", ms_emm)])
            ms_E = dves(lambda e, ac=ac: e.scalar_tensor_tensor(
                e_lo[:, ac * 24:(ac + 1) * 24], pb[2][:, 0:24], 1.0,
                e_hi[:, ac * 24:(ac + 1) * 24], op0=ALU.mult, op1=ALU.subtract))

        wlesb_rev = bass.AP(temp1, D - 1, [[D, 24], [-1, D]])
        dves(lambda e: e.tensor_copy(wlerev_hi[:, :], wlesb_rev),
            waits=[("c", ms_c_all)])
        dves(lambda e: e.scalar_tensor_tensor(
            wlerev_lo[:, :], wlesb_rev, 1.0, wlerev_hi[:, :],
            op0=ALU.mult, op1=ALU.subtract))
        ms_wledone = s.val["dve"]
        ms_blel = s.do("sync", lambda e: e.dma_start(blesb[:, :], bler),
                       waits=[("dve", ms_wledone)], inc="c", inc_n=16)
        blesb_rev = bass.AP(temp1, D - 1, [[D, 1], [-1, D]])
        dves(lambda e: e.tensor_copy(blerev_hi[:, :], blesb_rev),
            waits=[("c", ms_blel)])
        dves(lambda e: e.scalar_tensor_tensor(
            blerev_lo[:, :], blesb_rev, 1.0, blerev_hi[:, :],
            op0=ALU.mult, op1=ALU.subtract))
        dves(lambda e: e.tensor_copy(wg_hi[:, :], wgsb[:, :]))
        dves(lambda e: e.scalar_tensor_tensor(
            wg_lo[:, :], wgsb[:, :], 1.0, wg_hi[:, :],
            op0=ALU.mult, op1=ALU.subtract))
        dves(lambda e: e.tensor_copy(bg_hi[:, :], bgsb[:, :]))
        dves(lambda e: e.scalar_tensor_tensor(
            bg_lo[:, :], bgsb[:, :], 1.0, bg_hi[:, :],
            op0=ALU.mult, op1=ALU.subtract))
        dves(lambda e: e.tensor_copy(b2_hi[:, :], b2sb[:, :]))
        ms_setup_dve = dves(lambda e: e.scalar_tensor_tensor(
            b2_lo[:, :], b2sb[:, :], 1.0, b2_hi[:, :],
            op0=ALU.mult, op1=ALU.subtract))

        def tap(name, src_ap, waits):
            if _DBG:
                s.do("sync", lambda e, d=dbg[name], sr=src_ap: e.dma_start(d, sr),
                     waits=waits, inc="db", inc_n=16)

        def ftap(name, src_ap, waits):
            if _DBG:
                return s.do("sync", lambda e, d=dbg[name], sr=src_ap:
                            e.dma_start(d, sr), waits=waits, inc="db", inc_n=16)
            return 0

        tap("t_s0", s0sb[:, :], [("dve", ms_cnt)])
        tap("t_x0", x0sb[:, :], [("dve", ms_cnt)])
        tap("t_cnt", cnt5[:, :], [("dve", ms_D)])
        tap("t_crevh", crev_hi[:, :], [("db", ms_crev)])
        tap("t_crevl", crev_lo[:, :], [("db", ms_crev)])
        tap("t_kneg", knegsb[:, :], [("dve", ms_ksp)])
        tap("t_d", D128[:, :], [("dve", ms_d128)])
        tap("t_fu", fu[:, :], [("dve", ms_uv)])
        tap("t_fv", fv[:, :], [("dve", ms_uv)])
        tap("t_ehi", e_hi[:, :], [("dve", ms_E)])
        tap("t_elo", e_lo[:, :], [("dve", ms_E)])
        tap("t_wrevh", welrev_hi[:, :], [("dve", ms_setup_dve)])

        w1q = 0
        w2q = 0
        ms_l2tp_last = 0
        pending_stage0 = None
        w1S = ("w1a", "w1b", "w1c")
        w2S = ("w2a", "w2b", "w2c", "w2d")
        ms_w1_used = {}
        ms_w2_used = {}
        ms_gelu = {}
        ms_pbevac = {}
        pe_end = dve_end = act_end = 0

        for t in range(NTILE):
            XA = xa[t % 2]
            bar_pe = [("dve", dve_end), ("act", act_end)]
            bar_dve = [("act", act_end),
                       (xS[t % 2], 16 * (t // 2 + 1))]
            bar_act = [("dve", dve_end)]
            if t == 0:
                bar_pe = bar_pe + [("c", ms_c_all), ("db", ms_crev)]
                bar_dve = bar_dve + [("c", ms_c_all), ("db", ms_crev)]

            prev_l2tp = ms_l2tp_last
            ms_stage = [0] * NTT
            ms_tpw = [0] * NTT
            ms_ahi = [0] * NTT
            ms_x1mm = [0] * NTT
            ms_x1evac = [0] * NTT
            ms_lat = [0] * NTT
            ms_bits = [0] * NTT
            ms_seff = [0] * NTT
            ms_corr = [0] * NTT
            ms_synT_cp = [0] * NTT

            def emit_ln1_stage(tt, XA=XA, bar_dve=bar_dve, bar_act=bar_act,
                               pe_wait=0):
                c0 = tt * D
                dve(lambda e, c0=c0, XA=XA: e.reduce_sum(
                    stat[:, 0:1], XA[:, c0:c0 + D], axis=AX.X), waits=bar_dve)
                dves(lambda e: e.tensor_scalar_mul(stat[:, 0:1], stat[:, 0:1],
                                                   1.0 / D))
                dves(lambda e, c0=c0, XA=XA: e.tensor_scalar_sub(
                    temp1[:, :], XA[:, c0:c0 + D], stat[:, 0:1]))
                dve(lambda e: e.tensor_mul(xst_lo[:, :], temp1[:, :],
                                           temp1[:, :]),
                    waits=[("pe", pe_wait)])
                dve(lambda e: e.reduce_sum(stat[:, 1:2], xst_lo[:, :],
                                           axis=AX.X))
                ms_v = dves(lambda e: e.tensor_scalar(
                    stat[:, 2:3], stat[:, 1:2], 1.0 / D, 1e-5,
                    op0=ALU.mult, op1=ALU.add))
                act(lambda e: e.activation(stat[:, 3:4], stat[:, 2:3], AF.Ln),
                    waits=bar_act + [("dve", ms_v)])
                ms_r = acts(lambda e: e.activation(stat[:, 4:5], stat[:, 3:4],
                                                   AF.Exp, scale=-0.5))
                dve(lambda e: e.tensor_scalar_mul(
                    xst_hi[:, :], temp1[:, :], stat[:, 4:5]),
                    waits=[("act", ms_r)])
                return dve(lambda e: e.scalar_tensor_tensor(
                    xst_lo[:, :], temp1[:, :], stat[:, 4:5],
                    xst_hi[:, :], op0=ALU.mult, op1=ALU.subtract))

            for tt in range(NTT):
                c0 = tt * D
                sl = tt % 2
                if tt == 0 and pending_stage0 is not None:
                    ms_stage[0] = pending_stage0
                else:
                    ms_stage[tt] = emit_ln1_stage(
                        tt, pe_wait=(ms_tpw[tt - 1] if tt >= 1 else prev_l2tp))
                wave_ready_act = ms_ahi[tt - 2] if tt >= 2 else 0
                for src, dst in ((xst_hi, a_hi), (xst_lo, a_lo)):
                    for ac in range(8):
                        ms_tpw[tt] = pe(
                            lambda e, ac=ac, src=src: e.transpose(
                                pt[:, ac * 128:(ac + 1) * 128],
                                src[:, ac * 128:(ac + 1) * 128],
                                ident[:, :]),
                            waits=bar_pe + [("dve", ms_stage[tt]),
                                            ("act", wave_ready_act),
                                            ("dve", ms_synT_cp[tt - 2] if tt >= 2 else 0)],
                            inc="pe" if ac == 7 else None)
                    for ac in range(8):
                        ms_ahi[tt] = act(
                            lambda e, sl=sl, ac=ac, dst=dst: e.copy(
                                dst[sl][:, ac * 128:(ac + 1) * 128],
                                pt[:, ac * 128:(ac + 1) * 128]),
                            waits=[("pe", ms_tpw[tt])])
                    wave_ready_act = ms_ahi[tt]
                bp = 2 * (tt % 2)
                for db in range(2):
                    nib = 0
                    for wv in (blerev_hi, blerev_lo):
                        ms_x1mm[tt] = pe(
                            lambda e, db=db, wv=wv, bp=bp, st_f=(nib == 0):
                            e.matmul(pb[bp + db][:, :], ones[:, :],
                                     wv[:, db * 512:(db + 1) * 512],
                                     start=st_f, stop=False),
                            waits=[("act", ms_ahi[tt]),
                                   ("dve", ms_x1evac[tt - 2] if tt >= 2 else 0)],
                            inc=None)
                        nib += 1
                    for ac in range(8):
                        for lh, ch in ((a_hi, crev_hi), (a_hi, crev_lo),
                                       (a_lo, crev_hi)):
                            ms_x1mm[tt] = pe(
                                lambda e, db=db, ac=ac, lh=lh, ch=ch, sl=sl, bp=bp:
                                e.matmul(pb[bp + db][:, :],
                                         lh[sl][:, ac * 128:(ac + 1) * 128],
                                         ch[:, ac * D + db * 512:ac * D + (db + 1) * 512],
                                         start=False, stop=False),
                                inc="pe" if (ac == 7 and ch is crev_hi and lh is a_lo) else None)
                n_mm = 0
                for ac in range(8):
                    for lh, ev in ((a_hi, e_hi), (a_lo, e_hi), (a_hi, e_lo)):
                        ms_lat[tt] = pe(
                            lambda e, ac=ac, lh=lh, ev=ev, sl=sl,
                            st_f=(n_mm == 0), sp_f=(n_mm == 23):
                            e.matmul(pl[0:24, 0:128],
                                     ev[:, ac * 24:(ac + 1) * 24],
                                     lh[sl][:, ac * 128:(ac + 1) * 128],
                                     start=st_f, stop=sp_f),
                            waits=[("dve", ms_seff[tt - 1] if tt >= 1 else 0)],
                            inc="pe" if n_mm == 23 else None)
                        n_mm += 1
                ms_bits[tt] = dve(lambda e, sl=sl: e.tensor_scalar(
                    bitsT[sl][:, :], pl[0:24, 0:128], bel_sb[:24, 0:1], 0.0,
                    op0=ALU.add, op1=ALU.is_gt), waits=[("pe", ms_lat[tt])])
                ms_synmm = pe(lambda e, sl=sl: e.matmul(
                    pl[:, 128:140], bitsT[sl][:, :], htc[:, :],
                    start=True, stop=True),
                    waits=[("dve", ms_bits[tt])], inc="pe")
                dve(lambda e, sl=sl: e.tensor_single_scalar(
                    syn01[sl][:, :], pl[:, 128:140], 0.5, op=ALU.is_ge),
                    waits=[("pe", ms_synmm)])
                dves(lambda e, sl=sl: e.reduce_sum(synw[:, 0:1],
                                                   syn01[sl][:, :], axis=AX.X))
                dves(lambda e: e.tensor_scalar(synw[:, 1:2], synw[:, 0:1], 0.5,
                                               None, op0=ALU.is_ge))
                dves(lambda e: e.tensor_scalar(synw[:, 2:3], synw[:, 0:1], 3.5,
                                               None, op0=ALU.is_le))
                dves(lambda e: e.tensor_mul(synw[:, 3:4], synw[:, 1:2],
                                            synw[:, 2:3]))
                ms_seff[tt] = dves(lambda e, sl=sl: e.tensor_scalar_mul(
                    syneff[sl][:, :], syn01[sl][:, :], synw[:, 3:4]))
                ms_sT = pe(lambda e, sl=sl: e.transpose(
                    pt[0:12, 0:128], syneff[sl][:, :], ident[:, :]),
                    waits=[("dve", ms_seff[tt]), ("act", ms_ahi[tt])],
                    inc="pe")
                dve(lambda e, sl=sl: e.tensor_copy(synT[sl][:, :],
                                                   pt[0:12, 0:128]),
                    waits=[("pe", ms_sT)])
                ms_synT_cp[tt] = s.val["dve"]
                dve(lambda e, sl=sl: e.tensor_copy(corrT[sl][:, :],
                                                   bitsT[sl][:, :]))
                dves(lambda e, sl=sl: e.tensor_scalar(
                    corrT[sl][0:12, :], synT[sl][:, :], -2.0, 1.0,
                    op0=ALU.mult, op1=ALU.add))
                dves(lambda e, sl=sl: e.tensor_mul(
                    corrT[sl][0:12, :], corrT[sl][0:12, :], bitsT[sl][0:12, :]))
                ms_corr[tt] = dves(lambda e, sl=sl: e.tensor_add(
                    corrT[sl][0:12, :], corrT[sl][0:12, :], synT[sl][:, :]))
                for db in range(2):
                    for wi, wv in enumerate((wlerev_hi, wlerev_lo)):
                        ms_x1mm[tt] = pe(
                            lambda e, db=db, wv=wv, sl=sl, bp=bp, sp_f=(wi == 1):
                            e.matmul(pb[bp + db][:, :], corrT[sl][:, :],
                                     wv[:, db * 512:(db + 1) * 512],
                                     start=False, stop=sp_f),
                            waits=[("dve", ms_corr[tt])],
                            inc="pe" if (db == 1 and wi == 1) else None)
                for db in range(2):
                    ms_x1evac[tt] = dve(
                        lambda e, c0=c0, db=db, bp=bp, XA=XA: e.tensor_add(
                            XA[:, c0 + (1 - db) * 512:c0 + (2 - db) * 512],
                            XA[:, c0 + (1 - db) * 512:c0 + (2 - db) * 512],
                            bass.AP(pb[bp + db], 511, [[512, 128], [-1, 512]])),
                        waits=[("pe", ms_x1mm[tt])])

            ms_f1 = ftap("t_fxa1", XA[:, :],
                         [("dve", ms_x1evac[NTT - 1])]) if t == 0 else 0
            ms_l2evac = [0] * NTT
            ms_gmm = [0] * NTT
            ms_g = [0] * NTT
            ms_gt = [0] * NTT
            ms_b2mm = [0] * NTT
            ms_b2evac = [0] * NTT
            ms_l2tp = [0] * NTT
            for tt in range(NTT):
                c0 = tt * D
                sl = tt % 2
                dve(lambda e, c0=c0, XA=XA: e.reduce_sum(
                    stat[:, 0:1], XA[:, c0:c0 + D], axis=AX.X),
                    waits=[("db", ms_f1)])
                dves(lambda e: e.tensor_scalar_mul(stat[:, 0:1], stat[:, 0:1],
                                                   1.0 / D))
                dves(lambda e, c0=c0, XA=XA: e.tensor_scalar_sub(
                    temp1[:, :], XA[:, c0:c0 + D], stat[:, 0:1]))
                dve(lambda e: e.tensor_mul(xst_lo[:, :], temp1[:, :],
                                            temp1[:, :]),
                    waits=[("pe", ms_l2tp[tt - 1] if tt >= 1 else ms_tpw[NTT - 1])])
                dve(lambda e: e.reduce_sum(stat[:, 1:2], xst_lo[:, :],
                                           axis=AX.X))
                ms_v = dves(lambda e: e.tensor_scalar(
                    stat[:, 2:3], stat[:, 1:2], 1.0 / D, 1e-5,
                    op0=ALU.mult, op1=ALU.add))
                act(lambda e: e.activation(stat[:, 3:4], stat[:, 2:3], AF.Ln),
                    waits=[("dve", ms_v)])
                ms_r = acts(lambda e: e.activation(stat[:, 4:5], stat[:, 3:4],
                                                   AF.Exp, scale=-0.5))
                dve(lambda e: e.tensor_scalar_mul(
                    xst_hi[:, :], temp1[:, :], stat[:, 4:5]),
                    waits=[("act", ms_r)])
                ms_stg = dve(lambda e: e.scalar_tensor_tensor(
                    xst_lo[:, :], temp1[:, :], stat[:, 4:5],
                    xst_hi[:, :], op0=ALU.mult, op1=ALU.subtract))
                wav_act = ms_l2evac[tt - 2] if tt >= 2 else ms_ahi[NTT - 1]
                for ac in range(8):
                    ms_l2tp[tt] = pe(lambda e, ac=ac: e.transpose(
                        pt[:, ac * 128:(ac + 1) * 128],
                        xst_hi[:, ac * 128:(ac + 1) * 128], ident[:, :]),
                        waits=[("dve", ms_stg), ("act", wav_act)],
                        inc="pe" if ac == 7 else None)
                for ac in range(8):
                    ms_l2evac[tt] = act(lambda e, tt=tt, ac=ac: e.copy(
                        xlnT2[:, ac * TILE + tt * 128:ac * TILE + (tt + 1) * 128],
                        pt[:, ac * 128:(ac + 1) * 128]),
                        waits=[("pe", ms_l2tp[tt])])
                for ac in range(8):
                    ms_l2tp[tt] = pe(lambda e, ac=ac: e.transpose(
                        pt[:, ac * 128:(ac + 1) * 128],
                        xst_lo[:, ac * 128:(ac + 1) * 128], ident[:, :]),
                        waits=[("act", ms_l2evac[tt])],
                        inc="pe" if ac == 7 else None)
                for ac in range(8):
                    ms_l2evac[tt] = act(lambda e, sl=sl, ac=ac: e.copy(
                        l2lo[sl][:, ac * 128:(ac + 1) * 128],
                        pt[:, ac * 128:(ac + 1) * 128]),
                        waits=[("pe", ms_l2tp[tt])])
                n_mm = 0
                for wv in (bg_hi, bg_lo):
                    ms_gmm[tt] = pe(lambda e, wv=wv, st_f=(n_mm == 0):
                                    e.matmul(pl[:, 288:296], ones[:, :],
                                             wv[:, :], start=st_f, stop=False),
                                    waits=[("act", ms_l2evac[tt]),
                                           ("dve", ms_g[tt - 1] if tt >= 1 else 0)],
                                    inc=None)
                    n_mm += 1
                for ac in range(8):
                    for which in range(3):
                        sp_f = (n_mm == 25)
                        if which == 1:
                            fn = (lambda e, ac=ac, sl=sl, sp_f=sp_f:
                                  e.matmul(pl[:, 288:296],
                                           l2lo[sl][:, ac * 128:(ac + 1) * 128],
                                           wg_hi[:, ac * NE:(ac + 1) * NE],
                                           start=False, stop=sp_f))
                        else:
                            wv = wg_hi if which == 0 else wg_lo
                            fn = (lambda e, ac=ac, tt=tt, wv=wv, sp_f=sp_f:
                                  e.matmul(pl[:, 288:296],
                                           xlnT2[:, ac * TILE + tt * 128:ac * TILE + (tt + 1) * 128],
                                           wv[:, ac * NE:(ac + 1) * NE],
                                           start=False, stop=sp_f))
                        ms_gmm[tt] = pe(fn, inc="pe" if sp_f else None)
                        n_mm += 1
                dve(lambda e: e.tensor_copy(logit_sb[:, :], pl[:, 288:296]),
                    waits=[("pe", ms_gmm[tt])])
                dves(lambda e: e.reduce_max(gstat[:, 0:1], logit_sb[:, :],
                                            axis=AX.X))
                ms_lsh = dves(lambda e: e.tensor_scalar_sub(
                    gtmp[:, 0:NE], logit_sb[:, :], gstat[:, 0:1]))
                ms_exp = act(lambda e: e.activation(
                    gtmp[:, 3 * NE:4 * NE], gtmp[:, 0:NE], AF.Exp),
                    waits=[("dve", ms_lsh)])
                ms_sume = dves(lambda e: e.reduce_sum(
                    gstat[:, 1:2], gtmp[:, 3 * NE:4 * NE], axis=AX.X),
                    waits=[("act", ms_exp)])
                dves(lambda e: e.tensor_scalar(gtmp[:, NE:2 * NE],
                                               gtmp[:, 0:NE], 0.0, None,
                                               op0=ALU.is_ge))
                dves(lambda e: e.scalar_tensor_tensor(
                    gtmp[:, NE:2 * NE], gtmp[:, NE:2 * NE], -1e30,
                    gtmp[:, 0:NE], op0=ALU.mult, op1=ALU.add))
                dves(lambda e: e.reduce_max(gstat[:, 2:3], gtmp[:, NE:2 * NE],
                                            axis=AX.X))
                dves(lambda e: e.tensor_scalar(gtmp[:, 2 * NE:3 * NE],
                                               gtmp[:, 0:NE], gstat[:, 2:3],
                                               None, op0=ALU.is_ge))
                dves(lambda e: e.reciprocal(gstat[:, 3:4], gstat[:, 1:2]))
                dves(lambda e: e.tensor_mul(gtmp[:, 2 * NE:3 * NE],
                                            gtmp[:, 2 * NE:3 * NE],
                                            gtmp[:, 3 * NE:4 * NE]))
                dves(lambda e, tt=tt: e.tensor_scalar_mul(
                    G[:, tt * NE:(tt + 1) * NE], gtmp[:, 2 * NE:3 * NE],
                    gstat[:, 3:4]))
                ms_g[tt] = dves(lambda e, tt=tt: e.tensor_copy(
                    gbf[:, :], G[:, tt * NE:(tt + 1) * NE]))
                ms_gtp = pe(lambda e: e.transpose(pt[0:8, 0:128], gbf[:, :],
                                                  ident[:, :]),
                            waits=[("dve", ms_g[tt])], inc="pe")
                ms_gt[tt] = act(lambda e, tt=tt: e.copy(
                    gt_sb[:, tt * 128:(tt + 1) * 128], pt[0:8, 0:128]),
                    waits=[("pe", ms_gtp)])
                for db in range(2):
                    for wi, wv in enumerate((b2_hi, b2_lo)):
                        ms_b2mm[tt] = pe(
                            lambda e, db=db, wv=wv, tt=tt,
                            st_f=(wi == 0), sp_f=(wi == 1):
                            e.matmul(pb[2 + db][:, :],
                                     gt_sb[:, tt * 128:(tt + 1) * 128],
                                     wv[:, db * 512:(db + 1) * 512],
                                     start=st_f, stop=sp_f),
                            waits=[("act", ms_gt[tt]),
                                   ("dve", ms_b2evac[tt - 1] if tt >= 1
                                    else ms_x1evac[NTT - 1])],
                            inc="pe" if (db == 1 and wi == 1) else None)
                for db in range(2):
                    ms_b2evac[tt] = dve(lambda e, c0=c0, db=db, XA=XA:
                                        e.tensor_add(
                        XA[:, c0 + db * 512:c0 + (db + 1) * 512],
                        XA[:, c0 + db * 512:c0 + (db + 1) * 512],
                        pb[2 + db][:, :]), waits=[("pe", ms_b2mm[tt])])

            ms_f2 = ftap("t_fxa2", XA[:, :],
                         [("dve", ms_b2evac[NTT - 1])]) if t == 0 else 0
            if t + 1 < NTILE:
                nxt_bar_dve = [("act", s.val["act"]),
                               (xS[(t + 1) % 2], 16 * ((t + 1) // 2 + 1))]
                pending_stage0 = emit_ln1_stage(
                    0, XA=xa[(t + 1) % 2], bar_dve=nxt_bar_dve, bar_act=[],
                    pe_wait=ms_l2tp[NTT - 1])
            else:
                pending_stage0 = None

            ms_f4 = 0
            for e_ in range(NE):
                if t == 0:
                    emit_conv_range(conv_bounds[e_ - 1] if e_ > 0 else 0,
                                    conv_bounds[e_])
                for jc in range(32):
                    jg, jj = jc // 4, jc % 4
                    if jj == 0:
                        s.do("sync", lambda e, e_=e_, jg=jg, sl=w1q % W1R:
                             e.dma_start(w1buf[sl][:, :], w1bf[e_, jg]),
                             waits=(list(ms_conv[e_]) if t == 0 else []) +
                                   [("pe", ms_w1_used.get(w1q - W1R, 0))],
                             inc=w1S[w1q % W1R], inc_n=16)
                        w1q += 1
                    bank = 4 + (jc % 2)
                    if jc >= 2:
                        aw = ms_gelu[(e_, jc - 2)]
                    elif e_ > 0:
                        aw = ms_gelu[(e_ - 1, jc + 30)]
                    else:
                        aw = ms_l2evac[NTT - 1]
                    ser_w = []
                    if "a" in _SER and e_ > 0 and jc == 0:
                        ser_w.append(("dve", ms_pbevac[(e_ - 1, 1, NTT - 1)]))
                    if "b" in _SER and jc == 0:
                        ser_w.append(("act", ms_gelu[(e_ - 1, 31)] if e_ > 0 else 0))
                    ms_pa = 0
                    for dc in range(8):
                        ms_pa = pe(
                            lambda e, jc=jc, dc=dc, jj=jj, bank=bank,
                            sl=(w1q - 1) % W1R:
                            e.matmul(pb[bank][:, :],
                                     w1buf[sl][:, dc * 512 + jj * 128:dc * 512 + (jj + 1) * 128],
                                     xlnT2[:, dc * TILE:(dc + 1) * TILE],
                                     start=(dc == 0), stop=(dc == 7)),
                            waits=[(w1S[(w1q - 1) % W1R], 16 * ((w1q - 1) // W1R + 1)),
                                   ("act", aw),
                                   ("dve", ms_b2evac[NTT - 1] if (e_ == 0 and jc == 0) else 0),
                                   ("db", ms_f2 if (e_ == 0 and jc == 0)
                                    else (ms_f4 if (e_ == 1 and jc == 0) else 0))] + ser_w,
                            inc="pe" if dc == 7 else None)
                    ms_w1_used[w1q - 1] = ms_pa
                    ms_gelu[(e_, jc)] = act(
                        lambda e, e_=e_, jc=jc, bank=bank: e.activation(
                            hT[:, jc * TILE:(jc + 1) * TILE], pb[bank][:, :],
                            AF.Gelu,
                            bias=b1sb[:, e_ * 32 + jc:e_ * 32 + jc + 1]),
                        waits=[("pe", ms_pa)])
                ms_f3 = ftap("t_fht", hT[:, :],
                             [("act", ms_gelu[(e_, 31)])]) if (t == 0 and e_ == 0) else 0
                for p_ in range(2):
                    if e_ == 0 and p_ == 0:
                        dw = ms_b2evac[NTT - 1]
                    elif p_ == 1:
                        dw = ms_pbevac[(e_, 0, NTT - 1)]
                    else:
                        dw = ms_pbevac[(e_ - 1, 1, NTT - 1)]
                    ms_pb_mm = 0
                    for jg2 in range(8):
                        s.do("sync", lambda e, e_=e_, jg2=jg2, p_=p_, sl=w2q % W2R:
                             e.dma_start(w2buf[sl][:, :], w2bf[e_, p_, jg2]),
                             waits=(list(ms_conv[e_]) if t == 0 else []) +
                                   [("pe", max(ms_w2_used.get(w2q - W2R, 0),
                                               ms_kneg_mm))],
                             inc=w2S[w2q % W2R], inc_n=16)
                        w2q += 1
                        for tt in range(NTT):
                            for jj in range(4):
                                jcg = jg2 * 4 + jj
                                ms_pb_mm = pe(
                                    lambda e, jcg=jcg, tt=tt,
                                    sl=(w2q - 1) % W2R, jj=jj:
                                    e.matmul(pb[tt][:, :],
                                             hT[:, jcg * TILE + tt * 128:jcg * TILE + (tt + 1) * 128],
                                             w2buf[sl][:, jj * 512:(jj + 1) * 512],
                                             start=(jcg == 0), stop=(jcg == 31)),
                                    waits=[(w2S[(w2q - 1) % W2R],
                                            16 * ((w2q - 1) // W2R + 1)),
                                           ("act", ms_gelu[(e_, jg2 * 4 + 3)]),
                                           ("dve", dw), ("db", ms_f3)],
                                    inc="pe" if (tt == NTT - 1 and jj == 3) else None)
                        ms_w2_used[w2q - 1] = ms_pb_mm
                    for tt in range(NTT):
                        ms_pbevac[(e_, p_, tt)] = dve(
                            lambda e, tt=tt, p_=p_, e_=e_, XA=XA:
                            e.scalar_tensor_tensor(
                                XA[:, tt * D + p_ * 512:tt * D + p_ * 512 + 512],
                                pb[tt][:, :],
                                G[:, tt * NE + e_:tt * NE + e_ + 1],
                                XA[:, tt * D + p_ * 512:tt * D + p_ * 512 + 512],
                                op0=ALU.mult, op1=ALU.add),
                            waits=[("pe", ms_pb_mm)])

            if t == 0:
                tap("t_stat", stat[:, :], [("dve", ms_stage[0])])
                tap("t_junk", junkbf[:, :], [("dve", ms_stage[0])])
                tap("t_temp1", temp1[:, :], [("dve", ms_stage[0])])
                tap("t_xa_attn", XA[:, :], [("dve", ms_x1evac[NTT - 1])])
                tap("t_bits", bitsT[1][:, :], [("dve", ms_bits[3])])
                tap("t_corr", corrT[1][:, :], [("dve", ms_corr[3])])
                tap("t_ahi", a_hi[1][:, :], [("act", ms_ahi[3])])
                tap("t_xlnt2", xlnT2[:, :], [("act", ms_l2evac[NTT - 1])])
                tap("t_g", G[:, :], [("dve", ms_g[NTT - 1])])
                tap("t_ht", hT[:, :], [("act", ms_gelu[(0, 31)])])
                tap("t_w1b", w1buf[1][:, :], [("w1", 16 * 2)])

            ms_ffn_done = ms_pbevac[(NE - 1, 1, NTT - 1)]
            out_ap = out[t * TILE:(t + 1) * TILE, :].rearrange(
                "(a p) d -> p a d", p=128)
            src_ap = XA[:, :].rearrange("p (a d) -> p a d", a=NTT)
            s.do("gp", lambda e, d=out_ap, srcb=src_ap: e.dma_start(d, srcb),
                 waits=[("dve", ms_ffn_done)], inc="st", inc_n=16)
            if t + 2 < NTILE:
                s.do("gp", emit_xload(t + 2),
                     waits=[("st", 16 * (t + 1))], inc=xS[t % 2], inc_n=16)

            pe_end = s.val["pe"]
            dve_end = s.val["dve"]
            act_end = s.val["act"]
            ms_l2tp_last = ms_l2tp[NTT - 1]

        s.do("gp", lambda e: e.wait_ge(sems["st"], 16 * NTILE))

        with nc.Block() as block:
            def runner(key):
                def fn(eng):
                    for wl, f, rec in s.ops[key]:
                        for sname, v in wl:
                            eng.wait_ge(sems[sname], v)
                        ins = f(eng)
                        if rec is not None:
                            ins.then_inc(sems[rec[0]], rec[1])
                return fn

            block.tensor(runner("pe"))
            block.vector(runner("dve"))
            block.scalar(runner("act"))
            block.sync(runner("sync"))
            block.gpsimd(runner("gp"))

    return nc


_NC_CACHE = None


def kernel(L0_attn_x, L0_attn_W_el, L0_attn_b_el, L0_attn_W_le, L0_attn_b_le,
           L0_ffn_Wg, L0_ffn_bg, L0_ffn_W1, L0_ffn_b1, L0_ffn_W2, L0_ffn_b2):
    x = np.ascontiguousarray(np.asarray(L0_attn_x, dtype=np.float32))
    B, S, E = x.shape
    xf = x.reshape(-1, E)
    assert xf.shape[0] == NCORE * TPC and E == D

    jm = np.arange(D, dtype=np.int64)
    ang = 2.0 * np.pi * (np.outer(jm, jm) % D).astype(np.float64) / D
    cos_f32 = np.cos(ang).astype(np.float32)
    sin_f32 = np.sin(ang).astype(np.float32)
    cos_hi, cos_lo = _split_hl(cos_f32)
    sin_hi, sin_lo = _split_hl(sin_f32)
    ka = (np.abs(np.fft.fftfreq(D)).astype(np.float32) + np.float32(1e-8))
    ka2 = (ka * ka).astype(np.float32)
    kap = np.ascontiguousarray(ka.reshape(8, 128).T)
    kap2 = np.ascontiguousarray(ka2.reshape(8, 128).T)
    lx = np.log(np.array([2., 4., 8., 16., 32.],
                         dtype=np.float32)).astype(np.float64)
    den = 5.0 * (lx * lx).sum() - lx.sum() ** 2
    wc2 = (-(5.0 * lx - lx.sum()) / den).astype(np.float32).reshape(1, 5)
    identb = np.eye(128, dtype=bf16)
    jrevb = np.ascontiguousarray(np.eye(128, dtype=np.float32)[::-1]).astype(bf16)
    A_ = np.array([[1,1,1,0,1,1,1,0,0,0,1,1],[1,1,0,1,1,1,0,0,0,1,1,0],
                   [1,0,1,1,1,0,0,0,1,1,0,1],[0,1,1,1,0,0,0,1,1,0,1,1],
                   [0,0,0,0,1,1,1,1,1,1,1,1],[1,1,1,1,0,0,0,0,0,0,0,0],
                   [0,0,0,0,1,1,1,1,0,0,0,0],[1,1,0,0,1,1,0,0,1,1,0,0],
                   [1,0,1,0,1,0,1,0,1,0,1,0],[0,1,0,1,0,1,0,1,0,1,0,1],
                   [1,0,0,1,1,0,0,1,1,0,0,1],[0,1,1,0,0,1,1,0,0,1,1,0]],
                  dtype=np.float32)
    H = np.concatenate([A_.T, np.eye(12, dtype=np.float32)], axis=1)
    htb = np.ascontiguousarray(H.T).astype(bf16)
    onesb = np.ones((1, 128), dtype=bf16)

    w1 = np.ascontiguousarray(np.asarray(L0_ffn_W1, dtype=np.float32)
                              .reshape(NE, 8, 128, 8, 512)
                              .transpose(0, 3, 2, 1, 4))
    w2 = np.ascontiguousarray(np.asarray(L0_ffn_W2, dtype=np.float32)
                              .reshape(NE, 32, 128, D))
    b1 = np.ascontiguousarray(np.asarray(L0_ffn_b1, dtype=np.float32)
                              .reshape(NE, 32, 128).transpose(0, 2, 1))
    b2 = np.ascontiguousarray(np.asarray(L0_ffn_b2, dtype=np.float32))
    wel = np.ascontiguousarray(np.asarray(L0_attn_W_el, dtype=np.float32))
    bel = np.ascontiguousarray(np.asarray(L0_attn_b_el, dtype=np.float32)
                               .reshape(24, 1))
    wle = np.ascontiguousarray(np.asarray(L0_attn_W_le, dtype=np.float32))
    ble = np.ascontiguousarray(np.asarray(L0_attn_b_le, dtype=np.float32)
                               .reshape(1, D))
    wg = np.ascontiguousarray(np.asarray(L0_ffn_Wg, dtype=np.float32))
    bg = np.ascontiguousarray(np.asarray(L0_ffn_bg, dtype=np.float32)
                              .reshape(1, NE))
    x0 = np.ascontiguousarray(xf[0:1, :])

    global _NC_CACHE
    if _NC_CACHE is None:
        _NC_CACHE = build_nc()
    nc = _NC_CACHE

    common = dict(x0row=x0, w1r=w1, w2r=w2, welr=wel, belr=bel, wler=wle,
                  bler=ble, wgr=wg, bgr=bg, b1r=b1, b2r=b2,
                  ccos_hi=cos_hi, ccos_lo=cos_lo, csin_hi=sin_hi,
                  csin_lo=sin_lo, ckap=kap, ckap2=kap2, cwc2=wc2,
                  cident=identb, cht=htb, cones=onesb, cjrev=jrevb)
    in_maps = []
    for c in range(NCORE):
        m = dict(common)
        m["xs"] = np.ascontiguousarray(xf[c * TPC:(c + 1) * TPC, :])
        in_maps.append(m)

    res = run_bass_kernel_spmd(nc, in_maps, core_ids=list(range(NCORE)))
    outs = [np.asarray(r["out"], dtype=np.float32) for r in res.results]
    return np.concatenate(outs, axis=0).reshape(B, S, E)


# version 12
# speedup vs baseline: 1.1576x; 1.1576x over previous
import os
import numpy as np
import ml_dtypes
import concourse.bass as bass
import concourse.mybir as mybir
from concourse.bass_utils import run_bass_kernel_spmd
from contextlib import ExitStack

F32 = mybir.dt.float32
BF16 = mybir.dt.bfloat16
AF = mybir.ActivationFunctionType
ALU = mybir.AluOpType
AX = mybir.AxisListType
bf16 = ml_dtypes.bfloat16

NCORE = 8
D = 1024
NE = 8
TPC = 8192
TILE = 512
NTILE = TPC // TILE
NTT = TILE // 128
PI = float(np.pi)


def _split_hl(a):
    hi = a.astype(bf16)
    lo = (a.astype(np.float32) - hi.astype(np.float32)).astype(bf16)
    return np.ascontiguousarray(hi), np.ascontiguousarray(lo)


class Sched:
    ENGS = ("pe", "dve", "act", "sync", "gp")

    def __init__(self):
        self.ops = {k: [] for k in self.ENGS}
        self.val = {}
        self.lastw = {}

    def sem(self, name):
        self.val[name] = 0

    def do(self, eng, fn, waits=(), inc=None, inc_n=1):
        wl = []
        for sname, v in waits:
            if v is None or v <= 0:
                continue
            key = (eng, sname)
            if self.lastw.get(key, 0) >= v:
                continue
            self.lastw[key] = v
            wl.append((sname, v))
        rec = None
        if inc is not None:
            self.val[inc] += inc_n
            rec = (inc, inc_n)
        self.ops[eng].append((wl, fn, rec))
        return self.val[inc] if inc is not None else None


_DBG = os.environ.get("KDBG", "") == "1"
_SER = os.environ.get("KSER", "")


def build_nc():
    nc = bass.Bass("TRN2", target_bir_lowering=False)
    dbg = {}
    if _DBG:
        for nm, shape, dt_ in (
                ("t_crevh", [128, 8192], BF16), ("t_crevl", [128, 8192], BF16),
                ("t_kneg", [1, 1024], F32), ("t_d", [128, 1], F32),
                ("t_fu", [128, 8], F32), ("t_fv", [128, 8], F32),
                ("t_ehi", [128, 192], BF16), ("t_elo", [128, 192], BF16),
                ("t_wrevh", [128, 192], BF16),
                ("t_xa_attn", [128, 4096], F32),
                ("t_xlnt2", [128, 4096], BF16),
                ("t_g", [128, 32], F32),
                ("t_ht", [128, 16384], BF16),
                ("t_bits", [24, 128], BF16), ("t_corr", [24, 128], BF16),
                ("t_w1b", [128, 4096], BF16),
                ("t_ahi", [128, 1024], BF16),
                ("t_fxa1", [128, 4096], F32), ("t_fxa2", [128, 4096], F32),
                ("t_fht", [128, 16384], BF16), ("t_fxa3", [128, 4096], F32),
                ("t_s0", [1, 1024], F32), ("t_cnt", [1, 16], F32),
                ("t_stat", [128, 16], F32), ("t_x0", [1, 1024], F32),
                ("t_junk", [128, 1024], BF16), ("t_temp1", [128, 1024], F32)):
            dbg[nm] = nc.dram_tensor(nm, shape, dt_, kind="ExternalOutput").ap()

    xs = nc.dram_tensor("xs", [TPC, D], F32, kind="ExternalInput").ap()
    x0row = nc.dram_tensor("x0row", [1, D], F32, kind="ExternalInput").ap()
    w1r = nc.dram_tensor("w1r", [NE, 8, 128, 8, 512], F32, kind="ExternalInput").ap()
    w2r = nc.dram_tensor("w2r", [NE, 32, 128, D], F32, kind="ExternalInput").ap()
    welr = nc.dram_tensor("welr", [D, 24], F32, kind="ExternalInput").ap()
    belr = nc.dram_tensor("belr", [24, 1], F32, kind="ExternalInput").ap()
    wler = nc.dram_tensor("wler", [24, D], F32, kind="ExternalInput").ap()
    bler = nc.dram_tensor("bler", [1, D], F32, kind="ExternalInput").ap()
    wgr = nc.dram_tensor("wgr", [D, NE], F32, kind="ExternalInput").ap()
    bgr = nc.dram_tensor("bgr", [1, NE], F32, kind="ExternalInput").ap()
    b1r = nc.dram_tensor("b1r", [NE, 128, 32], F32, kind="ExternalInput").ap()
    b2r = nc.dram_tensor("b2r", [NE, D], F32, kind="ExternalInput").ap()
    ccos_hi = nc.dram_tensor("ccos_hi", [D, D], BF16, kind="ExternalInput").ap()
    ccos_lo = nc.dram_tensor("ccos_lo", [D, D], BF16, kind="ExternalInput").ap()
    csin_hi = nc.dram_tensor("csin_hi", [D, D], BF16, kind="ExternalInput").ap()
    csin_lo = nc.dram_tensor("csin_lo", [D, D], BF16, kind="ExternalInput").ap()
    ckap = nc.dram_tensor("ckap", [128, 8], F32, kind="ExternalInput").ap()
    ckap2 = nc.dram_tensor("ckap2", [128, 8], F32, kind="ExternalInput").ap()
    cwc2 = nc.dram_tensor("cwc2", [1, 5], F32, kind="ExternalInput").ap()
    cident = nc.dram_tensor("cident", [128, 128], BF16, kind="ExternalInput").ap()
    cht = nc.dram_tensor("cht", [24, 12], BF16, kind="ExternalInput").ap()
    cones = nc.dram_tensor("cones", [1, 128], BF16, kind="ExternalInput").ap()
    cjrev = nc.dram_tensor("cjrev", [128, 128], BF16, kind="ExternalInput").ap()
    out = nc.dram_tensor("out", [TPC, D], F32, kind="ExternalOutput").ap()
    w1bf = nc.dram_tensor("w1bf", [NE, 8, 128, 4096], BF16, kind="Internal").ap()
    w2bf = nc.dram_tensor("w2bf", [NE, 2, 8, 128, 2048], BF16, kind="Internal").ap()
    k2r_hi = nc.dram_tensor("k2r_hi", [2048], BF16, kind="Internal").ap()
    k2r_lo = nc.dram_tensor("k2r_lo", [2048], BF16, kind="Internal").ap()
    dbr = nc.dram_tensor("dbr", [128], F32, kind="Internal").ap()

    s = Sched()
    SEMNAMES = ("c", "cw", "db", "st", "gp", "pe", "dve", "act",
                "x0", "x1s",
                "w1a", "w1b", "w1c", "w1d", "w2a", "w2b", "w2c", "w2d", "w2e",
                "cla", "clb", "csa", "csb")
    for nm in SEMNAMES:
        s.sem(nm)

    with ExitStack() as es:
        def sbt(name, shape, dtype):
            return es.enter_context(nc.sbuf_tensor(name, shape, dtype))

        def psa(name, shape, dtype):
            return es.enter_context(nc.psum_tensor(name, shape, dtype))

        sems = {nm: es.enter_context(nc.semaphore(nm)) for nm in SEMNAMES}

        xa = [sbt("xa0", [128, NTT * D], F32), sbt("xa1", [128, NTT * D], F32)]
        xlnT2 = sbt("xlnT2", [128, 8 * TILE], BF16)
        a_hi = [sbt("a_hi0", [128, 8 * 128], BF16), sbt("a_hi1", [128, 8 * 128], BF16)]
        a_lo = [sbt("a_lo0", [128, 8 * 128], BF16), sbt("a_lo1", [128, 8 * 128], BF16)]
        l2lo = None
        hT = sbt("hT", [128, 32 * TILE], BF16)
        crev_hi = sbt("crev_hi", [128, 8 * D], BF16)
        crev_lo = sbt("crev_lo", [128, 8 * D], BF16)
        W1R, W2R = 3, 4
        w1buf = [sbt(f"w1b{i}", [128, 4096], BF16) for i in range(W1R)]
        w2buf = [sbt(f"w2b{i}", [128, 2048], BF16) for i in range(W2R)]
        xst_hi = sbt("xsth0", [128, D], BF16)
        xst_lo = sbt("xstl0", [128, D], BF16)
        temp1 = sbt("temp10", [128, D], F32)
        junkbf = temp1[32:33, :]
        stat = sbt("stat", [128, 16], F32)
        G = sbt("G", [128, NTT * NE], F32)
        gbf = sbt("gbf", [128, NE], BF16)
        gt_sb = sbt("gt_sb", [8, TILE], BF16)
        e_hi = sbt("e_hi", [128, 8 * 24], BF16)
        e_lo = sbt("e_lo", [128, 8 * 24], BF16)
        wlerev_hi = sbt("wlerev_hi", [24, D], BF16)
        wlerev_lo = sbt("wlerev_lo", [24, D], BF16)
        blerev_hi = sbt("blerev_hi", [1, D], BF16)
        blerev_lo = sbt("blerev_lo", [1, D], BF16)
        bel_sb = sbt("bel_sb", [24, 1], F32)
        wg_hi = sbt("wg_hi", [128, 8 * NE], BF16)
        wg_lo = sbt("wg_lo", [128, 8 * NE], BF16)
        bg_hi = sbt("bg_hi", [1, NE], BF16)
        bg_lo = sbt("bg_lo", [1, NE], BF16)
        b1sb = sbt("b1sb", [128, NE * 32], F32)
        b2_hi = sbt("b2_hi", [8, D], BF16)
        b2_lo = sbt("b2_lo", [8, D], BF16)
        ident = sbt("ident", [128, 128], BF16)
        jrev = sbt("jrev", [128, 128], BF16)
        htc = sbt("htc", [24, 12], BF16)
        ones = sbt("ones", [1, 128], BF16)
        bitsT = [sbt("bitsT0", [24, 128], BF16), sbt("bitsT1", [24, 128], BF16)]
        syn01 = [sbt("syn010", [128, 12], BF16), sbt("syn011", [128, 12], BF16)]
        synw = sbt("synw", [128, 8], F32)
        syneff = [sbt("syneff0", [128, 12], BF16), sbt("syneff1", [128, 12], BF16)]
        synT = [sbt("synT0", [12, 128], BF16), sbt("synT1", [12, 128], BF16)]
        corrT = [sbt("corrT0", [24, 128], BF16), sbt("corrT1", [24, 128], BF16)]
        logit_sb = sbt("logit_sb", [128, NE], F32)
        gtmp = sbt("gtmp", [128, 4 * NE], F32)
        gstat = sbt("gstat", [128, 8], F32)

        l2lo = a_lo
        pb = [psa(f"pb{i}", [128, 512], F32) for i in range(6)]
        pt = psa("pt", [128, 1024], BF16)
        pl = psa("pl", [128, 512], F32)

        x0sb = sbt("x0sb", [1, D], F32)
        s0sb = x0sb
        cnt5 = sbt("cnt5", [1, 16], F32)
        wc2sb = sbt("wc2sb", [1, 8], F32)
        dstat = sbt("dstat", [1, 8], F32)
        Dsb = sbt("Dsb", [1, 1], F32)
        D128 = sbt("D128", [128, 1], F32)
        dcoef = sbt("dcoef", [128, 8], F32)
        kap = sbt("kap", [128, 8], F32)
        kap2 = sbt("kap2", [128, 8], F32)
        farg = sbt("farg", [128, 8], F32)
        fphase = sbt("fphase", [128, 8], F32)
        freal = sbt("freal", [128, 8], F32)
        fcos = sbt("fcos", [128, 8], F32)
        fsin = sbt("fsin", [128, 8], F32)
        fu = sbt("fu", [128, 8], F32)
        fv = sbt("fv", [128, 8], F32)
        u_hi = sbt("u_hi", [128, 8], BF16)
        u_lo = sbt("u_lo", [128, 8], BF16)
        v_hi = sbt("v_hi", [128, 8], BF16)
        v_lo = sbt("v_lo", [128, 8], BF16)
        cosbuf = w2buf[0]
        knegsb = x0sb
        kneg_hi = hT[0:1, 4 * D:5 * D]
        kneg_lo = hT[0:1, 5 * D:6 * D]
        welsb = sbt("welsb", [128, 8 * 24], F32)
        wel_hi = sbt("wel_hi", [128, 8 * 24], BF16)
        wel_lo = sbt("wel_lo", [128, 8 * 24], BF16)
        welrev_hi = sbt("welrev_hi", [128, 8 * 24], BF16)
        welrev_lo = sbt("welrev_lo", [128, 8 * 24], BF16)
        wlesb = temp1[0:24, :]
        blesb = temp1[0:1, :]
        wgsb = sbt("wgsb", [128, 8 * NE], F32)
        bgsb = sbt("bgsb", [1, NE], F32)
        b2sb = sbt("b2sb", [8, D], F32)
        tpbuf_hi = sbt("tpbuf_hi", [128, 128], BF16)
        tpbuf_lo = sbt("tpbuf_lo", [128, 128], BF16)
        stage32 = [sbt("stg320", [128, 1024], F32), sbt("stg321", [128, 1024], F32)]
        stagebf = [sbt("stgbf0", [128, 1024], BF16), sbt("stgbf1", [128, 1024], BF16)]

        def dve(fn, waits=()):
            return s.do("dve", fn, waits=waits, inc="dve")

        def dves(fn, waits=()):
            return s.do("dve", fn,
                        waits=list(waits) + [("dve", s.val["dve"])], inc="dve")

        def act(fn, waits=()):
            return s.do("act", fn, waits=waits, inc="act")

        def acts(fn, waits=()):
            return s.do("act", fn,
                        waits=list(waits) + [("act", s.val["act"])], inc="act")

        def pe(fn, waits=(), inc=None):
            return s.do("pe", fn, waits=waits, inc=inc)

        def cload(dst, src):
            return s.do("sync", lambda e, d=dst, sr=src: e.dma_start(d, sr),
                        inc="c", inc_n=16)

        cload(ident[:, :], cident)
        cload(jrev[:, :], cjrev)
        cload(htc[:, :], cht)
        cload(ones[:, :], cones)
        cload(kap[:, :], ckap)
        cload(kap2[:, :], ckap2)
        cload(wc2sb[:, :5], cwc2)
        ms_x0 = cload(x0sb[:, :], x0row)
        for dc in range(8):
            cload(welsb[:, dc * 24:(dc + 1) * 24], welr[dc * 128:(dc + 1) * 128, :])
        cload(wlesb[:, :], wler)
        cload(bel_sb[:, :], belr)
        for dc in range(8):
            cload(wgsb[:, dc * NE:(dc + 1) * NE], wgr[dc * 128:(dc + 1) * 128, :])
        cload(bgsb[:, :], bgr)
        for ee in range(NE):
            cload(b1sb[:, ee * 32:(ee + 1) * 32], b1r[ee])
        ms_c_all = cload(b2sb[:, :], b2r)

        def emit_xload(t):
            ap_in = xs[t * TILE:(t + 1) * TILE, :].rearrange(
                "(a p) d -> p a d", p=128)
            dst = xa[t % 2][:, :].rearrange("p (a d) -> p a d", a=NTT)
            return lambda e, d=dst, sr=ap_in: e.dma_start(d, sr)

        xS = ("x0", "x1s")
        s.do("gp", emit_xload(0), inc=xS[0], inc_n=16)
        s.do("gp", emit_xload(1), inc=xS[1], inc_n=16)

        ms_conv = [0] * NE
        conv_chunks = []
        conv_bounds = []
        for ee in range(NE):
            for jg in range(8):
                w1flat = w1r[ee, jg].rearrange("p a b -> p (a b)")
                for k4 in range(4):
                    conv_chunks.append((w1flat[:, k4 * D:(k4 + 1) * D],
                                        w1bf[ee, jg][:, k4 * D:(k4 + 1) * D]))
            for jc in range(32):
                conv_chunks.append((w2r[ee, jc], [
                    w2bf[ee, db, jc // 4, :, (jc % 4) * 512:(jc % 4) * 512 + 512]
                    for db in range(2)]))
            conv_bounds.append(len(conv_chunks))
        nch = len(conv_chunks)
        clS = ("cla", "clb")
        csS = ("csa", "csb")

        def emit_conv_range(q0, q1):
            for q in range(q0, q1 + 1):
                if q < q1:
                    wv = 0
                    if q >= 2:
                        wv = sum((32 if isinstance(conv_chunks[i][1], list)
                                  else 16)
                                 for i in range(q % 2, q - 1, 2))
                    s.do("sync", lambda e, sl=q % 2, src=conv_chunks[q][0]:
                         e.dma_start(stage32[sl][:, :], src),
                         waits=[(csS[q % 2], wv)],
                         inc=clS[q % 2], inc_n=16)
                if q >= 1 and (q - 1) >= q0:
                    cp = s.do("gp", lambda e, sl=(q - 1) % 2:
                              e.tensor_copy(stagebf[sl][:, :], stage32[sl][:, :]),
                              waits=[(clS[(q - 1) % 2], 16 * ((q - 1) // 2 + 1))],
                              inc="gp")
                    dsts = conv_chunks[q - 1][1]
                    if not isinstance(dsts, list):
                        s.do("sync", lambda e, sl=(q - 1) % 2, dst=dsts:
                             e.dma_start(dst, stagebf[sl][:, :]),
                             waits=[("gp", cp)], inc=csS[(q - 1) % 2], inc_n=16)
                    else:
                        s.do("sync", lambda e, sl=(q - 1) % 2, dst=dsts[0]:
                             e.dma_start(dst, stagebf[sl][:, 0:512]),
                             waits=[("gp", cp)], inc=csS[(q - 1) % 2], inc_n=16)
                        s.do("sync", lambda e, sl=(q - 1) % 2, dst=dsts[1]:
                             e.dma_start(dst, stagebf[sl][:, 512:1024]),
                             inc=csS[(q - 1) % 2], inc_n=16)

        def _cs_counts(nb):
            c = [0, 0]
            for q in range(nb):
                kind = conv_chunks[q][1]
                c[q % 2] += 32 if isinstance(kind, list) else 16
            return c
        ms_conv = []
        for ee in range(NE):
            c0, c1 = _cs_counts(conv_bounds[ee])
            ms_conv.append(((csS[0], c0), (csS[1], c1)))

        dves(lambda e: e.reduce_sum(dstat[:, 0:1], x0sb[:, :], axis=AX.X),
            waits=[("c", ms_c_all)])
        dves(lambda e: e.tensor_scalar_mul(dstat[:, 0:1], dstat[:, 0:1], 1.0 / D))
        dves(lambda e: e.tensor_scalar_sub(s0sb[:, :], x0sb[:, :], dstat[:, 0:1]))
        dves(lambda e: e.tensor_mul(junkbf[:, :], s0sb[:, :], s0sb[:, :]))
        dves(lambda e: e.reduce_sum(dstat[:, 1:2], junkbf[:, :], axis=AX.X))
        ms_v0 = dves(lambda e: e.tensor_scalar(
            dstat[:, 2:3], dstat[:, 1:2], 1.0 / D, 1e-5,
            op0=ALU.mult, op1=ALU.add))
        acts(lambda e: e.activation(dstat[:, 3:4], dstat[:, 2:3], AF.Ln),
            waits=[("dve", ms_v0)])
        ms_r0 = acts(lambda e: e.activation(dstat[:, 4:5], dstat[:, 3:4],
                                           AF.Exp, scale=-0.5))
        dves(lambda e: e.tensor_scalar_mul(s0sb[:, :], s0sb[:, :], dstat[:, 4:5]),
            waits=[("act", ms_r0)])
        for i, n in enumerate((4, 16, 64, 256, 1024)):
            dves(lambda e, n=n: e.reduce_sum(dstat[:, 5:6], s0sb[:, :n], axis=AX.X))
            dves(lambda e, n=n: e.tensor_scalar_mul(dstat[:, 5:6],
                                                   dstat[:, 5:6], 1.0 / n))
            dves(lambda e, n=n: e.tensor_single_scalar(
                junkbf[:, :n], s0sb[:, :n], dstat[:1, 5:6], op=ALU.is_gt))
            dves(lambda e, n=n, i=i: e.reduce_sum(
                cnt5[:, i:i + 1], junkbf[:, :n], axis=AX.X))
        ms_cnt = dves(lambda e: e.tensor_scalar_max(cnt5[:, :5], cnt5[:, :5], 1.0))
        ms_ly = acts(lambda e: e.activation(cnt5[:, 8:13], cnt5[:, :5], AF.Ln),
                    waits=[("dve", ms_cnt)])
        dves(lambda e: e.tensor_mul(cnt5[:, 8:13], cnt5[:, 8:13], wc2sb[:, :5]),
            waits=[("act", ms_ly)])
        dves(lambda e: e.reduce_sum(Dsb[:, :], cnt5[:, 8:13], axis=AX.X))
        dves(lambda e: e.tensor_scalar_max(Dsb[:, :], Dsb[:, :], 1.0))
        ms_D = dves(lambda e: e.tensor_scalar_min(Dsb[:, :], Dsb[:, :], 2.0))

        dves(lambda e: e.tensor_copy(u_hi[0:1, 0:1], Dsb[:, :]),
            waits=[("dve", ms_D) if False else ("c", ms_c_all)])
        ms_dsp = dves(lambda e: e.scalar_tensor_tensor(
            u_lo[0:1, 0:1], Dsb[:, :], 1.0, u_hi[0:1, 0:1],
            op0=ALU.mult, op1=ALU.subtract))
        mdb1 = pe(lambda e: e.matmul(pb[3][:, 0:1], ones[:, :],
                                     u_hi[0:1, 0:1], start=True, stop=False),
                  waits=[("dve", ms_dsp), ("c", ms_c_all)], inc=None)
        mdb2 = pe(lambda e: e.matmul(pb[3][:, 0:1], ones[:, :],
                                     u_lo[0:1, 0:1], start=False, stop=True),
                  inc="pe")
        ms_d128 = dves(lambda e: e.tensor_copy(D128[:, :], pb[3][:, 0:1]),
                      waits=[("pe", mdb2)])

        dves(lambda e: e.tensor_scalar(dcoef[:, 0:1], D128[:, :], D128[:, 0:1],
                                      None, op0=ALU.mult),
            waits=[("c", ms_c_all)])
        dves(lambda e: e.tensor_scalar(dcoef[:, 1:2], dcoef[:, 0:1], 0.2,
                                      D128[:, 0:1], op0=ALU.mult, op1=ALU.add))
        dves(lambda e: e.tensor_scalar(dcoef[:, 2:3], D128[:, :], 0.5, 1.0,
                                      op0=ALU.mult, op1=ALU.add))
        dves(lambda e: e.tensor_scalar_mul(dcoef[:, 3:4], D128[:, :], 0.1))
        dves(lambda e: e.tensor_scalar(farg[:, :], kap[:, :], dcoef[:, 2:3],
                                      dcoef[:, 1:2], op0=ALU.mult, op1=ALU.add))
        dves(lambda e: e.tensor_scalar(fphase[:, :], kap2[:, :], dcoef[:, 3:4],
                                      dcoef[:, 1:2], op0=ALU.mult, op1=ALU.add))
        dves(lambda e: e.tensor_sub(fphase[:, :], fphase[:, :], kap[:, :]))
        dves(lambda e: e.tensor_scalar(fcos[:, :], farg[:, :], PI, None,
                                      op0=ALU.is_gt))
        ms_arg = dves(lambda e: e.scalar_tensor_tensor(
            farg[:, :], fcos[:, :], -2.0 * PI, farg[:, :],
            op0=ALU.mult, op1=ALU.add))
        ms_arg2 = dves(lambda e: e.tensor_scalar(
            fu[:, :], fphase[:, :], -1.0, PI / 2, op0=ALU.mult, op1=ALU.add))
        acts(lambda e: e.activation(freal[:, :], farg[:, :], AF.Sin),
            waits=[("dve", ms_arg2)])
        acts(lambda e: e.activation(fcos[:, :], fu[:, :], AF.Sin))
        ms_sin = acts(lambda e: e.activation(fsin[:, :], fphase[:, :], AF.Sin))
        dves(lambda e: e.tensor_mul(fu[:, :], freal[:, :], fcos[:, :]),
            waits=[("act", ms_sin)])
        dves(lambda e: e.tensor_mul(fv[:, :], freal[:, :], fsin[:, :]))
        dves(lambda e: e.tensor_copy(u_hi[:, :], fu[:, :]))
        dves(lambda e: e.scalar_tensor_tensor(u_lo[:, :], fu[:, :], 1.0,
                                             u_hi[:, :], op0=ALU.mult,
                                             op1=ALU.subtract))
        dves(lambda e: e.tensor_copy(v_hi[:, :], fv[:, :]))
        ms_uv = dves(lambda e: e.scalar_tensor_tensor(
            v_lo[:, :], fv[:, :], 1.0, v_hi[:, :],
            op0=ALU.mult, op1=ALU.subtract))

        ms_kneg_mm = 0
        ncw = 0
        for jc in range(8):
            for half, (shi, slo, vh, vl) in enumerate(
                    ((ccos_hi, ccos_lo, u_hi, u_lo),
                     (csin_hi, csin_lo, v_hi, v_lo))):
                for q, src in enumerate((shi, slo)):
                    s.do("sync", lambda e, q=q, src=src, jc=jc:
                         e.dma_start(cosbuf[:, q * D:(q + 1) * D],
                                     src[jc * 128:(jc + 1) * 128, :]),
                         waits=[("pe", ms_kneg_mm)], inc="cw", inc_n=16)
                    ncw += 1
                ops3 = [(vh, 0), (vl, 0), (vh, 1)]
                for mb in range(2):
                    for oi, (usrc, q) in enumerate(ops3):
                        st_f = (jc == 0 and half == 0) and oi == 0
                        sp_f = (jc == 7 and half == 1) and oi == 2
                        ms_kneg_mm = pe(
                            lambda e, usrc=usrc, q=q, jc=jc, mb=mb,
                            st_f=st_f, sp_f=sp_f:
                            e.matmul(pb[mb][0:1, :], usrc[:, jc:jc + 1],
                                     cosbuf[:, q * D + mb * 512:q * D + (mb + 1) * 512],
                                     start=st_f, stop=sp_f),
                            waits=[("cw", 16 * ncw), ("dve", ms_uv)],
                            inc="pe" if (oi == 2 and mb == 1) else None)
        dves(lambda e: e.tensor_scalar_mul(knegsb[:, 0:512], pb[0][0:1, :], 1.0 / D),
            waits=[("pe", ms_kneg_mm)])
        dves(lambda e: e.tensor_scalar_mul(knegsb[:, 512:1024], pb[1][0:1, :], 1.0 / D))
        dves(lambda e: e.tensor_copy(kneg_hi[:, :], knegsb[:, :]))
        ms_ksp = dves(lambda e: e.scalar_tensor_tensor(
            kneg_lo[:, :], knegsb[:, :], 1.0, kneg_hi[:, :],
            op0=ALU.mult, op1=ALU.subtract))

        ms_k2 = 0
        for dst, hi in ((k2r_hi, kneg_hi), (k2r_lo, kneg_lo)):
            for off in (0, 1024):
                ms_k2 = s.do("sync", lambda e, dst=dst, hi=hi, off=off:
                             e.dma_start(dst[off:off + 1023], hi[0:1, 1:1024]),
                             waits=[("dve", ms_ksp)], inc="db", inc_n=16)
                ms_k2 = s.do("sync", lambda e, dst=dst, hi=hi, off=off:
                             e.dma_start(dst[off + 1023:off + 1024], hi[0:1, 0:1]),
                             inc="db", inc_n=16)
        ms_crev = ms_k2
        for ac in range(8):
            ms_crev = s.do("sync", lambda e, ac=ac: e.dma_start(
                crev_hi[:, ac * D:(ac + 1) * D],
                bass.AP(k2r_hi.tensor, ac * 128, [[1, 128], [1, D]])),
                waits=[("db", ms_k2)], inc="db", inc_n=16)
            ms_crev = s.do("sync", lambda e, ac=ac: e.dma_start(
                crev_lo[:, ac * D:(ac + 1) * D],
                bass.AP(k2r_lo.tensor, ac * 128, [[1, 128], [1, D]])),
                inc="db", inc_n=16)

        dves(lambda e: e.tensor_copy(wel_hi[:, :], welsb[:, :]),
            waits=[("c", ms_c_all)])
        ms_welsp = dves(lambda e: e.scalar_tensor_tensor(
            wel_lo[:, :], welsb[:, :], 1.0, wel_hi[:, :],
            op0=ALU.mult, op1=ALU.subtract))
        ms_wrev = 0
        for bc in range(8):
            src_c = 7 - bc
            m1 = pe(lambda e, src_c=src_c: e.matmul(
                pl[:, 0:24], jrev[:, :], wel_hi[:, src_c * 24:(src_c + 1) * 24],
                start=True, stop=True),
                waits=[("dve", ms_welsp), ("c", ms_c_all)], inc="pe")
            d1 = dves(lambda e, bc=bc: e.tensor_copy(
                welrev_hi[:, bc * 24:(bc + 1) * 24], pl[:, 0:24]),
                waits=[("pe", m1)])
            m2 = pe(lambda e, src_c=src_c: e.matmul(
                pl[:, 24:48], jrev[:, :], wel_lo[:, src_c * 24:(src_c + 1) * 24],
                start=True, stop=True), waits=[("dve", d1)], inc="pe")
            ms_wrev = dves(lambda e, bc=bc: e.tensor_copy(
                welrev_lo[:, bc * 24:(bc + 1) * 24], pl[:, 24:48]),
                waits=[("pe", m2)])

        ms_E = 0
        for ac in range(8):
            n_mm = 0
            ms_emm = 0
            for bc in range(8):
                mt1 = pe(lambda e, ac=ac, bc=bc: e.transpose(
                    pt[:, 0:128],
                    crev_hi[:, ac * D + bc * 128:ac * D + (bc + 1) * 128],
                    ident[:, :]),
                    waits=[("db", ms_crev), ("dve", ms_wrev)], inc="pe")
                e1 = acts(lambda e: e.copy(tpbuf_hi[:, :], pt[:, 0:128]),
                         waits=[("pe", mt1)])
                mt2 = pe(lambda e, ac=ac, bc=bc: e.transpose(
                    pt[:, 128:256],
                    crev_lo[:, ac * D + bc * 128:ac * D + (bc + 1) * 128],
                    ident[:, :]), waits=[("act", e1)], inc="pe")
                e2 = acts(lambda e: e.copy(tpbuf_lo[:, :], pt[:, 128:256]),
                         waits=[("pe", mt2)])
                for tb, wv in ((tpbuf_hi, welrev_hi), (tpbuf_lo, welrev_hi),
                               (tpbuf_hi, welrev_lo)):
                    ms_emm = pe(lambda e, tb=tb, wv=wv, bc=bc,
                                st_f=(n_mm == 0), sp_f=(n_mm == 23):
                                e.matmul(pb[2][:, 0:24], tb[:, :],
                                         wv[:, bc * 24:(bc + 1) * 24],
                                         start=st_f, stop=sp_f),
                                waits=[("act", e2)],
                                inc="pe" if n_mm == 23 else None)
                    n_mm += 1
            dves(lambda e, ac=ac: e.tensor_copy(
                e_hi[:, ac * 24:(ac + 1) * 24], pb[2][:, 0:24]),
                waits=[("pe", ms_emm)])
            ms_E = dves(lambda e, ac=ac: e.scalar_tensor_tensor(
                e_lo[:, ac * 24:(ac + 1) * 24], pb[2][:, 0:24], 1.0,
                e_hi[:, ac * 24:(ac + 1) * 24], op0=ALU.mult, op1=ALU.subtract))

        wlesb_rev = bass.AP(temp1, D - 1, [[D, 24], [-1, D]])
        dves(lambda e: e.tensor_copy(wlerev_hi[:, :], wlesb_rev),
            waits=[("c", ms_c_all)])
        dves(lambda e: e.scalar_tensor_tensor(
            wlerev_lo[:, :], wlesb_rev, 1.0, wlerev_hi[:, :],
            op0=ALU.mult, op1=ALU.subtract))
        ms_wledone = s.val["dve"]
        ms_blel = s.do("sync", lambda e: e.dma_start(blesb[:, :], bler),
                       waits=[("dve", ms_wledone)], inc="c", inc_n=16)
        blesb_rev = bass.AP(temp1, D - 1, [[D, 1], [-1, D]])
        dves(lambda e: e.tensor_copy(blerev_hi[:, :], blesb_rev),
            waits=[("c", ms_blel)])
        dves(lambda e: e.scalar_tensor_tensor(
            blerev_lo[:, :], blesb_rev, 1.0, blerev_hi[:, :],
            op0=ALU.mult, op1=ALU.subtract))
        dves(lambda e: e.tensor_copy(wg_hi[:, :], wgsb[:, :]))
        dves(lambda e: e.scalar_tensor_tensor(
            wg_lo[:, :], wgsb[:, :], 1.0, wg_hi[:, :],
            op0=ALU.mult, op1=ALU.subtract))
        dves(lambda e: e.tensor_copy(bg_hi[:, :], bgsb[:, :]))
        dves(lambda e: e.scalar_tensor_tensor(
            bg_lo[:, :], bgsb[:, :], 1.0, bg_hi[:, :],
            op0=ALU.mult, op1=ALU.subtract))
        dves(lambda e: e.tensor_copy(b2_hi[:, :], b2sb[:, :]))
        ms_setup_dve = dves(lambda e: e.scalar_tensor_tensor(
            b2_lo[:, :], b2sb[:, :], 1.0, b2_hi[:, :],
            op0=ALU.mult, op1=ALU.subtract))

        def tap(name, src_ap, waits):
            if _DBG:
                s.do("sync", lambda e, d=dbg[name], sr=src_ap: e.dma_start(d, sr),
                     waits=waits, inc="db", inc_n=16)

        def ftap(name, src_ap, waits):
            if _DBG:
                return s.do("sync", lambda e, d=dbg[name], sr=src_ap:
                            e.dma_start(d, sr), waits=waits, inc="db", inc_n=16)
            return 0

        tap("t_s0", s0sb[:, :], [("dve", ms_cnt)])
        tap("t_x0", x0sb[:, :], [("dve", ms_cnt)])
        tap("t_cnt", cnt5[:, :], [("dve", ms_D)])
        tap("t_crevh", crev_hi[:, :], [("db", ms_crev)])
        tap("t_crevl", crev_lo[:, :], [("db", ms_crev)])
        tap("t_kneg", knegsb[:, :], [("dve", ms_ksp)])
        tap("t_d", D128[:, :], [("dve", ms_d128)])
        tap("t_fu", fu[:, :], [("dve", ms_uv)])
        tap("t_fv", fv[:, :], [("dve", ms_uv)])
        tap("t_ehi", e_hi[:, :], [("dve", ms_E)])
        tap("t_elo", e_lo[:, :], [("dve", ms_E)])
        tap("t_wrevh", welrev_hi[:, :], [("dve", ms_setup_dve)])

        w1q = 0
        w2q = 0
        ms_l2tp_last = 0
        pending_stage0 = None
        w1S = ("w1a", "w1b", "w1c")
        w2S = ("w2a", "w2b", "w2c", "w2d", "w2e")
        ms_w1_used = {}
        ms_w2_used = {}
        ms_gelu = {}
        ms_pbevac = {}
        pe_end = dve_end = act_end = 0

        for t in range(NTILE):
            XA = xa[t % 2]
            bar_pe = [("dve", dve_end), ("act", act_end)]
            bar_dve = [("act", act_end),
                       (xS[t % 2], 16 * (t // 2 + 1))]
            bar_act = [("dve", dve_end)]
            if t == 0:
                bar_pe = bar_pe + [("c", ms_c_all), ("db", ms_crev)]
                bar_dve = bar_dve + [("c", ms_c_all), ("db", ms_crev)]

            prev_l2tp = ms_l2tp_last
            ms_stage = [0] * NTT
            ms_tpw = [0] * NTT
            ms_ahi = [0] * NTT
            ms_x1mm = [0] * NTT
            ms_x1evac = [0] * NTT
            ms_lat = [0] * NTT
            ms_bits = [0] * NTT
            ms_seff = [0] * NTT
            ms_corr = [0] * NTT
            ms_synT_cp = [0] * NTT

            def emit_ln1_stage(tt, XA=XA, bar_dve=bar_dve, bar_act=bar_act,
                               pe_wait=0):
                c0 = tt * D
                dve(lambda e, c0=c0, XA=XA: e.reduce_sum(
                    stat[:, 0:1], XA[:, c0:c0 + D], axis=AX.X), waits=bar_dve)
                dves(lambda e: e.tensor_scalar_mul(stat[:, 0:1], stat[:, 0:1],
                                                   1.0 / D))
                dves(lambda e, c0=c0, XA=XA: e.tensor_scalar_sub(
                    temp1[:, :], XA[:, c0:c0 + D], stat[:, 0:1]))
                dve(lambda e: e.tensor_mul(xst_lo[:, :], temp1[:, :],
                                           temp1[:, :]),
                    waits=[("pe", pe_wait)])
                dve(lambda e: e.reduce_sum(stat[:, 1:2], xst_lo[:, :],
                                           axis=AX.X))
                ms_v = dves(lambda e: e.tensor_scalar(
                    stat[:, 2:3], stat[:, 1:2], 1.0 / D, 1e-5,
                    op0=ALU.mult, op1=ALU.add))
                act(lambda e: e.activation(stat[:, 3:4], stat[:, 2:3], AF.Ln),
                    waits=bar_act + [("dve", ms_v)])
                ms_r = acts(lambda e: e.activation(stat[:, 4:5], stat[:, 3:4],
                                                   AF.Exp, scale=-0.5))
                dve(lambda e: e.tensor_scalar_mul(
                    xst_hi[:, :], temp1[:, :], stat[:, 4:5]),
                    waits=[("act", ms_r)])
                return dve(lambda e: e.scalar_tensor_tensor(
                    xst_lo[:, :], temp1[:, :], stat[:, 4:5],
                    xst_hi[:, :], op0=ALU.mult, op1=ALU.subtract))

            for tt in range(NTT):
                c0 = tt * D
                sl = tt % 2
                if tt == 0 and pending_stage0 is not None:
                    ms_stage[0] = pending_stage0
                else:
                    ms_stage[tt] = emit_ln1_stage(
                        tt, pe_wait=(ms_tpw[tt - 1] if tt >= 1 else prev_l2tp))
                wave_ready_act = ms_ahi[tt - 2] if tt >= 2 else 0
                for src, dst in ((xst_hi, a_hi), (xst_lo, a_lo)):
                    for ac in range(8):
                        ms_tpw[tt] = pe(
                            lambda e, ac=ac, src=src: e.transpose(
                                pt[:, ac * 128:(ac + 1) * 128],
                                src[:, ac * 128:(ac + 1) * 128],
                                ident[:, :]),
                            waits=bar_pe + [("dve", ms_stage[tt]),
                                            ("act", wave_ready_act),
                                            ("dve", ms_synT_cp[tt - 2] if tt >= 2 else 0)],
                            inc="pe" if ac == 7 else None)
                    for ac in range(8):
                        ms_ahi[tt] = act(
                            lambda e, sl=sl, ac=ac, dst=dst: e.copy(
                                dst[sl][:, ac * 128:(ac + 1) * 128],
                                pt[:, ac * 128:(ac + 1) * 128]),
                            waits=[("pe", ms_tpw[tt])])
                    wave_ready_act = ms_ahi[tt]
                bp = 2 * (tt % 2)
                for db in range(2):
                    nib = 0
                    for wv in (blerev_hi, blerev_lo):
                        ms_x1mm[tt] = pe(
                            lambda e, db=db, wv=wv, bp=bp, st_f=(nib == 0):
                            e.matmul(pb[bp + db][:, :], ones[:, :],
                                     wv[:, db * 512:(db + 1) * 512],
                                     start=st_f, stop=False),
                            waits=[("act", ms_ahi[tt]),
                                   ("dve", ms_x1evac[tt - 2] if tt >= 2 else 0)],
                            inc=None)
                        nib += 1
                    for ac in range(8):
                        for lh, ch in ((a_hi, crev_hi), (a_hi, crev_lo),
                                       (a_lo, crev_hi)):
                            ms_x1mm[tt] = pe(
                                lambda e, db=db, ac=ac, lh=lh, ch=ch, sl=sl, bp=bp:
                                e.matmul(pb[bp + db][:, :],
                                         lh[sl][:, ac * 128:(ac + 1) * 128],
                                         ch[:, ac * D + db * 512:ac * D + (db + 1) * 512],
                                         start=False, stop=False),
                                inc="pe" if (ac == 7 and ch is crev_hi and lh is a_lo) else None)
                n_mm = 0
                for ac in range(8):
                    for lh, ev in ((a_hi, e_hi), (a_lo, e_hi), (a_hi, e_lo)):
                        ms_lat[tt] = pe(
                            lambda e, ac=ac, lh=lh, ev=ev, sl=sl,
                            st_f=(n_mm == 0), sp_f=(n_mm == 23):
                            e.matmul(pl[0:24, 0:128],
                                     ev[:, ac * 24:(ac + 1) * 24],
                                     lh[sl][:, ac * 128:(ac + 1) * 128],
                                     start=st_f, stop=sp_f),
                            waits=[("dve", ms_seff[tt - 1] if tt >= 1 else 0)],
                            inc="pe" if n_mm == 23 else None)
                        n_mm += 1
                ms_bits[tt] = dve(lambda e, sl=sl: e.tensor_scalar(
                    bitsT[sl][:, :], pl[0:24, 0:128], bel_sb[:24, 0:1], 0.0,
                    op0=ALU.add, op1=ALU.is_gt), waits=[("pe", ms_lat[tt])])
                ms_synmm = pe(lambda e, sl=sl: e.matmul(
                    pl[:, 128:140], bitsT[sl][:, :], htc[:, :],
                    start=True, stop=True),
                    waits=[("dve", ms_bits[tt])], inc="pe")
                dve(lambda e, sl=sl: e.tensor_single_scalar(
                    syn01[sl][:, :], pl[:, 128:140], 0.5, op=ALU.is_ge),
                    waits=[("pe", ms_synmm)])
                dves(lambda e, sl=sl: e.reduce_sum(synw[:, 0:1],
                                                   syn01[sl][:, :], axis=AX.X))
                dves(lambda e: e.tensor_scalar(synw[:, 1:2], synw[:, 0:1], 0.5,
                                               None, op0=ALU.is_ge))
                dves(lambda e: e.tensor_scalar(synw[:, 2:3], synw[:, 0:1], 3.5,
                                               None, op0=ALU.is_le))
                dves(lambda e: e.tensor_mul(synw[:, 3:4], synw[:, 1:2],
                                            synw[:, 2:3]))
                ms_seff[tt] = dves(lambda e, sl=sl: e.tensor_scalar_mul(
                    syneff[sl][:, :], syn01[sl][:, :], synw[:, 3:4]))
                ms_sT = pe(lambda e, sl=sl: e.transpose(
                    pt[0:12, 0:128], syneff[sl][:, :], ident[:, :]),
                    waits=[("dve", ms_seff[tt]), ("act", ms_ahi[tt])],
                    inc="pe")
                dve(lambda e, sl=sl: e.tensor_copy(synT[sl][:, :],
                                                   pt[0:12, 0:128]),
                    waits=[("pe", ms_sT)])
                ms_synT_cp[tt] = s.val["dve"]
                dve(lambda e, sl=sl: e.tensor_copy(corrT[sl][:, :],
                                                   bitsT[sl][:, :]))
                dves(lambda e, sl=sl: e.tensor_scalar(
                    corrT[sl][0:12, :], synT[sl][:, :], -2.0, 1.0,
                    op0=ALU.mult, op1=ALU.add))
                dves(lambda e, sl=sl: e.tensor_mul(
                    corrT[sl][0:12, :], corrT[sl][0:12, :], bitsT[sl][0:12, :]))
                ms_corr[tt] = dves(lambda e, sl=sl: e.tensor_add(
                    corrT[sl][0:12, :], corrT[sl][0:12, :], synT[sl][:, :]))
                for db in range(2):
                    for wi, wv in enumerate((wlerev_hi, wlerev_lo)):
                        ms_x1mm[tt] = pe(
                            lambda e, db=db, wv=wv, sl=sl, bp=bp, sp_f=(wi == 1):
                            e.matmul(pb[bp + db][:, :], corrT[sl][:, :],
                                     wv[:, db * 512:(db + 1) * 512],
                                     start=False, stop=sp_f),
                            waits=[("dve", ms_corr[tt])],
                            inc="pe" if (db == 1 and wi == 1) else None)
                for db in range(2):
                    ms_x1evac[tt] = dve(
                        lambda e, c0=c0, db=db, bp=bp, XA=XA: e.tensor_add(
                            XA[:, c0 + (1 - db) * 512:c0 + (2 - db) * 512],
                            XA[:, c0 + (1 - db) * 512:c0 + (2 - db) * 512],
                            bass.AP(pb[bp + db], 511, [[512, 128], [-1, 512]])),
                        waits=[("pe", ms_x1mm[tt])])

            ms_f1 = ftap("t_fxa1", XA[:, :],
                         [("dve", ms_x1evac[NTT - 1])]) if t == 0 else 0
            ms_l2evac = [0] * NTT
            ms_gmm = [0] * NTT
            ms_g = [0] * NTT
            ms_gt = [0] * NTT
            ms_b2mm = [0] * NTT
            ms_b2evac = [0] * NTT
            ms_l2tp = [0] * NTT
            for tt in range(NTT):
                c0 = tt * D
                sl = tt % 2
                dve(lambda e, c0=c0, XA=XA: e.reduce_sum(
                    stat[:, 0:1], XA[:, c0:c0 + D], axis=AX.X),
                    waits=[("db", ms_f1)])
                dves(lambda e: e.tensor_scalar_mul(stat[:, 0:1], stat[:, 0:1],
                                                   1.0 / D))
                dves(lambda e, c0=c0, XA=XA: e.tensor_scalar_sub(
                    temp1[:, :], XA[:, c0:c0 + D], stat[:, 0:1]))
                dve(lambda e: e.tensor_mul(xst_lo[:, :], temp1[:, :],
                                            temp1[:, :]),
                    waits=[("pe", ms_l2tp[tt - 1] if tt >= 1 else ms_tpw[NTT - 1])])
                dve(lambda e: e.reduce_sum(stat[:, 1:2], xst_lo[:, :],
                                           axis=AX.X))
                ms_v = dves(lambda e: e.tensor_scalar(
                    stat[:, 2:3], stat[:, 1:2], 1.0 / D, 1e-5,
                    op0=ALU.mult, op1=ALU.add))
                act(lambda e: e.activation(stat[:, 3:4], stat[:, 2:3], AF.Ln),
                    waits=[("dve", ms_v)])
                ms_r = acts(lambda e: e.activation(stat[:, 4:5], stat[:, 3:4],
                                                   AF.Exp, scale=-0.5))
                dve(lambda e: e.tensor_scalar_mul(
                    xst_hi[:, :], temp1[:, :], stat[:, 4:5]),
                    waits=[("act", ms_r)])
                ms_stg = dve(lambda e: e.scalar_tensor_tensor(
                    xst_lo[:, :], temp1[:, :], stat[:, 4:5],
                    xst_hi[:, :], op0=ALU.mult, op1=ALU.subtract))
                wav_act = ms_l2evac[tt - 2] if tt >= 2 else ms_ahi[NTT - 1]
                for ac in range(8):
                    ms_l2tp[tt] = pe(lambda e, ac=ac: e.transpose(
                        pt[:, ac * 128:(ac + 1) * 128],
                        xst_hi[:, ac * 128:(ac + 1) * 128], ident[:, :]),
                        waits=[("dve", ms_stg), ("act", wav_act)],
                        inc="pe" if ac == 7 else None)
                for ac in range(8):
                    ms_l2evac[tt] = act(lambda e, tt=tt, ac=ac: e.copy(
                        xlnT2[:, ac * TILE + tt * 128:ac * TILE + (tt + 1) * 128],
                        pt[:, ac * 128:(ac + 1) * 128]),
                        waits=[("pe", ms_l2tp[tt])])
                for ac in range(8):
                    ms_l2tp[tt] = pe(lambda e, ac=ac: e.transpose(
                        pt[:, ac * 128:(ac + 1) * 128],
                        xst_lo[:, ac * 128:(ac + 1) * 128], ident[:, :]),
                        waits=[("act", ms_l2evac[tt])],
                        inc="pe" if ac == 7 else None)
                for ac in range(8):
                    ms_l2evac[tt] = act(lambda e, sl=sl, ac=ac: e.copy(
                        l2lo[sl][:, ac * 128:(ac + 1) * 128],
                        pt[:, ac * 128:(ac + 1) * 128]),
                        waits=[("pe", ms_l2tp[tt])])
                n_mm = 0
                for wv in (bg_hi, bg_lo):
                    ms_gmm[tt] = pe(lambda e, wv=wv, st_f=(n_mm == 0):
                                    e.matmul(pl[:, 288:296], ones[:, :],
                                             wv[:, :], start=st_f, stop=False),
                                    waits=[("act", ms_l2evac[tt]),
                                           ("dve", ms_g[tt - 1] if tt >= 1 else 0)],
                                    inc=None)
                    n_mm += 1
                for ac in range(8):
                    for which in range(3):
                        sp_f = (n_mm == 25)
                        if which == 1:
                            fn = (lambda e, ac=ac, sl=sl, sp_f=sp_f:
                                  e.matmul(pl[:, 288:296],
                                           l2lo[sl][:, ac * 128:(ac + 1) * 128],
                                           wg_hi[:, ac * NE:(ac + 1) * NE],
                                           start=False, stop=sp_f))
                        else:
                            wv = wg_hi if which == 0 else wg_lo
                            fn = (lambda e, ac=ac, tt=tt, wv=wv, sp_f=sp_f:
                                  e.matmul(pl[:, 288:296],
                                           xlnT2[:, ac * TILE + tt * 128:ac * TILE + (tt + 1) * 128],
                                           wv[:, ac * NE:(ac + 1) * NE],
                                           start=False, stop=sp_f))
                        ms_gmm[tt] = pe(fn, inc="pe" if sp_f else None)
                        n_mm += 1
                dve(lambda e: e.tensor_copy(logit_sb[:, :], pl[:, 288:296]),
                    waits=[("pe", ms_gmm[tt])])
                dves(lambda e: e.reduce_max(gstat[:, 0:1], logit_sb[:, :],
                                            axis=AX.X))
                ms_lsh = dves(lambda e: e.tensor_scalar_sub(
                    gtmp[:, 0:NE], logit_sb[:, :], gstat[:, 0:1]))
                ms_exp = act(lambda e: e.activation(
                    gtmp[:, 3 * NE:4 * NE], gtmp[:, 0:NE], AF.Exp),
                    waits=[("dve", ms_lsh)])
                ms_sume = dves(lambda e: e.reduce_sum(
                    gstat[:, 1:2], gtmp[:, 3 * NE:4 * NE], axis=AX.X),
                    waits=[("act", ms_exp)])
                dves(lambda e: e.tensor_scalar(gtmp[:, NE:2 * NE],
                                               gtmp[:, 0:NE], 0.0, None,
                                               op0=ALU.is_ge))
                dves(lambda e: e.scalar_tensor_tensor(
                    gtmp[:, NE:2 * NE], gtmp[:, NE:2 * NE], -1e30,
                    gtmp[:, 0:NE], op0=ALU.mult, op1=ALU.add))
                dves(lambda e: e.reduce_max(gstat[:, 2:3], gtmp[:, NE:2 * NE],
                                            axis=AX.X))
                dves(lambda e: e.tensor_scalar(gtmp[:, 2 * NE:3 * NE],
                                               gtmp[:, 0:NE], gstat[:, 2:3],
                                               None, op0=ALU.is_ge))
                dves(lambda e: e.reciprocal(gstat[:, 3:4], gstat[:, 1:2]))
                dves(lambda e: e.tensor_mul(gtmp[:, 2 * NE:3 * NE],
                                            gtmp[:, 2 * NE:3 * NE],
                                            gtmp[:, 3 * NE:4 * NE]))
                dves(lambda e, tt=tt: e.tensor_scalar_mul(
                    G[:, tt * NE:(tt + 1) * NE], gtmp[:, 2 * NE:3 * NE],
                    gstat[:, 3:4]))
                ms_g[tt] = dves(lambda e, tt=tt: e.tensor_copy(
                    gbf[:, :], G[:, tt * NE:(tt + 1) * NE]))
                ms_gtp = pe(lambda e: e.transpose(pt[0:8, 0:128], gbf[:, :],
                                                  ident[:, :]),
                            waits=[("dve", ms_g[tt])], inc="pe")
                ms_gt[tt] = act(lambda e, tt=tt: e.copy(
                    gt_sb[:, tt * 128:(tt + 1) * 128], pt[0:8, 0:128]),
                    waits=[("pe", ms_gtp)])
                for db in range(2):
                    for wi, wv in enumerate((b2_hi, b2_lo)):
                        ms_b2mm[tt] = pe(
                            lambda e, db=db, wv=wv, tt=tt,
                            st_f=(wi == 0), sp_f=(wi == 1):
                            e.matmul(pb[2 + db][:, :],
                                     gt_sb[:, tt * 128:(tt + 1) * 128],
                                     wv[:, db * 512:(db + 1) * 512],
                                     start=st_f, stop=sp_f),
                            waits=[("act", ms_gt[tt]),
                                   ("dve", ms_b2evac[tt - 1] if tt >= 1
                                    else ms_x1evac[NTT - 1])],
                            inc="pe" if (db == 1 and wi == 1) else None)
                for db in range(2):
                    ms_b2evac[tt] = dve(lambda e, c0=c0, db=db, XA=XA:
                                        e.tensor_add(
                        XA[:, c0 + db * 512:c0 + (db + 1) * 512],
                        XA[:, c0 + db * 512:c0 + (db + 1) * 512],
                        pb[2 + db][:, :]), waits=[("pe", ms_b2mm[tt])])

            ms_f2 = ftap("t_fxa2", XA[:, :],
                         [("dve", ms_b2evac[NTT - 1])]) if t == 0 else 0
            if t + 1 < NTILE:
                nxt_bar_dve = [("act", s.val["act"]),
                               (xS[(t + 1) % 2], 16 * ((t + 1) // 2 + 1))]
                pending_stage0 = emit_ln1_stage(
                    0, XA=xa[(t + 1) % 2], bar_dve=nxt_bar_dve, bar_act=[],
                    pe_wait=ms_l2tp[NTT - 1])
            else:
                pending_stage0 = None

            ms_f4 = 0
            for e_ in range(NE):
                if t == 0:
                    emit_conv_range(conv_bounds[e_ - 1] if e_ > 0 else 0,
                                    conv_bounds[e_])
                for jc in range(32):
                    jg, jj = jc // 4, jc % 4
                    if jj == 0:
                        s.do("sync", lambda e, e_=e_, jg=jg, sl=w1q % W1R:
                             e.dma_start(w1buf[sl][:, :], w1bf[e_, jg]),
                             waits=(list(ms_conv[e_]) if t == 0 else []) +
                                   [("pe", ms_w1_used.get(w1q - W1R, 0))],
                             inc=w1S[w1q % W1R], inc_n=16)
                        w1q += 1
                    bank = 4 + (jc % 2)
                    if jc >= 2:
                        aw = ms_gelu[(e_, jc - 2)]
                    elif e_ > 0:
                        aw = ms_gelu[(e_ - 1, jc + 30)]
                    else:
                        aw = ms_l2evac[NTT - 1]
                    ser_w = []
                    if "a" in _SER and e_ > 0 and jc == 0:
                        ser_w.append(("dve", ms_pbevac[(e_ - 1, 1, NTT - 1)]))
                    if "b" in _SER and jc == 0:
                        ser_w.append(("act", ms_gelu[(e_ - 1, 31)] if e_ > 0 else 0))
                    ms_pa = 0
                    for dc in range(8):
                        ms_pa = pe(
                            lambda e, jc=jc, dc=dc, jj=jj, bank=bank,
                            sl=(w1q - 1) % W1R:
                            e.matmul(pb[bank][:, :],
                                     w1buf[sl][:, dc * 512 + jj * 128:dc * 512 + (jj + 1) * 128],
                                     xlnT2[:, dc * TILE:(dc + 1) * TILE],
                                     start=(dc == 0), stop=(dc == 7)),
                            waits=[(w1S[(w1q - 1) % W1R], 16 * ((w1q - 1) // W1R + 1)),
                                   ("act", aw),
                                   ("dve", ms_b2evac[NTT - 1] if (e_ == 0 and jc == 0) else 0),
                                   ("db", ms_f2 if (e_ == 0 and jc == 0)
                                    else (ms_f4 if (e_ == 1 and jc == 0) else 0))] + ser_w,
                            inc="pe" if dc == 7 else None)
                    ms_w1_used[w1q - 1] = ms_pa
                    ms_gelu[(e_, jc)] = act(
                        lambda e, e_=e_, jc=jc, bank=bank: e.activation(
                            hT[:, jc * TILE:(jc + 1) * TILE], pb[bank][:, :],
                            AF.Gelu,
                            bias=b1sb[:, e_ * 32 + jc:e_ * 32 + jc + 1]),
                        waits=[("pe", ms_pa)])
                ms_f3 = ftap("t_fht", hT[:, :],
                             [("act", ms_gelu[(e_, 31)])]) if (t == 0 and e_ == 0) else 0
                for p_ in range(2):
                    if e_ == 0 and p_ == 0:
                        dw = ms_b2evac[NTT - 1]
                    elif p_ == 1:
                        dw = ms_pbevac[(e_, 0, NTT - 1)]
                    else:
                        dw = ms_pbevac[(e_ - 1, 1, NTT - 1)]
                    ms_pb_mm = 0
                    ms_pb_tt = [0] * NTT
                    for jg2 in range(8):
                        s.do("sync", lambda e, e_=e_, jg2=jg2, p_=p_, sl=w2q % W2R:
                             e.dma_start(w2buf[sl][:, :], w2bf[e_, p_, jg2]),
                             waits=(list(ms_conv[e_]) if t == 0 else []) +
                                   [("pe", max(ms_w2_used.get(w2q - W2R, 0),
                                               ms_kneg_mm))],
                             inc=w2S[w2q % W2R], inc_n=16)
                        w2q += 1
                        for tt in range(NTT):
                            for jj in range(4):
                                jcg = jg2 * 4 + jj
                                do_inc = ((jg2 == 7 and jj == 3) or
                                          (tt == NTT - 1 and jj == 3))
                                r = pe(
                                    lambda e, jcg=jcg, tt=tt,
                                    sl=(w2q - 1) % W2R, jj=jj:
                                    e.matmul(pb[tt][:, :],
                                             hT[:, jcg * TILE + tt * 128:jcg * TILE + (tt + 1) * 128],
                                             w2buf[sl][:, jj * 512:(jj + 1) * 512],
                                             start=(jcg == 0), stop=(jcg == 31)),
                                    waits=[(w2S[(w2q - 1) % W2R],
                                            16 * ((w2q - 1) // W2R + 1)),
                                           ("act", ms_gelu[(e_, jg2 * 4 + 3)]),
                                           ("dve", dw), ("db", ms_f3)],
                                    inc="pe" if do_inc else None)
                                if do_inc:
                                    ms_pb_mm = r
                                    if jg2 == 7 and jj == 3:
                                        ms_pb_tt[tt] = r
                        ms_w2_used[w2q - 1] = ms_pb_mm
                    for tt in range(NTT):
                        ms_pbevac[(e_, p_, tt)] = dve(
                            lambda e, tt=tt, p_=p_, e_=e_, XA=XA:
                            e.scalar_tensor_tensor(
                                XA[:, tt * D + p_ * 512:tt * D + p_ * 512 + 512],
                                pb[tt][:, :],
                                G[:, tt * NE + e_:tt * NE + e_ + 1],
                                XA[:, tt * D + p_ * 512:tt * D + p_ * 512 + 512],
                                op0=ALU.mult, op1=ALU.add),
                            waits=[("pe", ms_pb_tt[tt])])

            if t == 0:
                tap("t_stat", stat[:, :], [("dve", ms_stage[0])])
                tap("t_junk", junkbf[:, :], [("dve", ms_stage[0])])
                tap("t_temp1", temp1[:, :], [("dve", ms_stage[0])])
                tap("t_xa_attn", XA[:, :], [("dve", ms_x1evac[NTT - 1])])
                tap("t_bits", bitsT[1][:, :], [("dve", ms_bits[3])])
                tap("t_corr", corrT[1][:, :], [("dve", ms_corr[3])])
                tap("t_ahi", a_hi[1][:, :], [("act", ms_ahi[3])])
                tap("t_xlnt2", xlnT2[:, :], [("act", ms_l2evac[NTT - 1])])
                tap("t_g", G[:, :], [("dve", ms_g[NTT - 1])])
                tap("t_ht", hT[:, :], [("act", ms_gelu[(0, 31)])])
                tap("t_w1b", w1buf[1][:, :], [("w1", 16 * 2)])

            ms_ffn_done = ms_pbevac[(NE - 1, 1, NTT - 1)]
            out_ap = out[t * TILE:(t + 1) * TILE, :].rearrange(
                "(a p) d -> p a d", p=128)
            src_ap = XA[:, :].rearrange("p (a d) -> p a d", a=NTT)
            s.do("gp", lambda e, d=out_ap, srcb=src_ap: e.dma_start(d, srcb),
                 waits=[("dve", ms_ffn_done)], inc="st", inc_n=16)
            if t + 2 < NTILE:
                s.do("gp", emit_xload(t + 2),
                     waits=[("st", 16 * (t + 1))], inc=xS[t % 2], inc_n=16)

            pe_end = s.val["pe"]
            dve_end = s.val["dve"]
            act_end = s.val["act"]
            ms_l2tp_last = ms_l2tp[NTT - 1]

        s.do("gp", lambda e: e.wait_ge(sems["st"], 16 * NTILE))

        with nc.Block() as block:
            def runner(key):
                def fn(eng):
                    for wl, f, rec in s.ops[key]:
                        for sname, v in wl:
                            eng.wait_ge(sems[sname], v)
                        ins = f(eng)
                        if rec is not None:
                            ins.then_inc(sems[rec[0]], rec[1])
                return fn

            block.tensor(runner("pe"))
            block.vector(runner("dve"))
            block.scalar(runner("act"))
            block.sync(runner("sync"))
            block.gpsimd(runner("gp"))

    return nc


_NC_CACHE = None


def kernel(L0_attn_x, L0_attn_W_el, L0_attn_b_el, L0_attn_W_le, L0_attn_b_le,
           L0_ffn_Wg, L0_ffn_bg, L0_ffn_W1, L0_ffn_b1, L0_ffn_W2, L0_ffn_b2):
    x = np.ascontiguousarray(np.asarray(L0_attn_x, dtype=np.float32))
    B, S, E = x.shape
    xf = x.reshape(-1, E)
    assert xf.shape[0] == NCORE * TPC and E == D

    jm = np.arange(D, dtype=np.int64)
    ang = 2.0 * np.pi * (np.outer(jm, jm) % D).astype(np.float64) / D
    cos_f32 = np.cos(ang).astype(np.float32)
    sin_f32 = np.sin(ang).astype(np.float32)
    cos_hi, cos_lo = _split_hl(cos_f32)
    sin_hi, sin_lo = _split_hl(sin_f32)
    ka = (np.abs(np.fft.fftfreq(D)).astype(np.float32) + np.float32(1e-8))
    ka2 = (ka * ka).astype(np.float32)
    kap = np.ascontiguousarray(ka.reshape(8, 128).T)
    kap2 = np.ascontiguousarray(ka2.reshape(8, 128).T)
    lx = np.log(np.array([2., 4., 8., 16., 32.],
                         dtype=np.float32)).astype(np.float64)
    den = 5.0 * (lx * lx).sum() - lx.sum() ** 2
    wc2 = (-(5.0 * lx - lx.sum()) / den).astype(np.float32).reshape(1, 5)
    identb = np.eye(128, dtype=bf16)
    jrevb = np.ascontiguousarray(np.eye(128, dtype=np.float32)[::-1]).astype(bf16)
    A_ = np.array([[1,1,1,0,1,1,1,0,0,0,1,1],[1,1,0,1,1,1,0,0,0,1,1,0],
                   [1,0,1,1,1,0,0,0,1,1,0,1],[0,1,1,1,0,0,0,1,1,0,1,1],
                   [0,0,0,0,1,1,1,1,1,1,1,1],[1,1,1,1,0,0,0,0,0,0,0,0],
                   [0,0,0,0,1,1,1,1,0,0,0,0],[1,1,0,0,1,1,0,0,1,1,0,0],
                   [1,0,1,0,1,0,1,0,1,0,1,0],[0,1,0,1,0,1,0,1,0,1,0,1],
                   [1,0,0,1,1,0,0,1,1,0,0,1],[0,1,1,0,0,1,1,0,0,1,1,0]],
                  dtype=np.float32)
    H = np.concatenate([A_.T, np.eye(12, dtype=np.float32)], axis=1)
    htb = np.ascontiguousarray(H.T).astype(bf16)
    onesb = np.ones((1, 128), dtype=bf16)

    w1 = np.ascontiguousarray(np.asarray(L0_ffn_W1, dtype=np.float32)
                              .reshape(NE, 8, 128, 8, 512)
                              .transpose(0, 3, 2, 1, 4))
    w2 = np.ascontiguousarray(np.asarray(L0_ffn_W2, dtype=np.float32)
                              .reshape(NE, 32, 128, D))
    b1 = np.ascontiguousarray(np.asarray(L0_ffn_b1, dtype=np.float32)
                              .reshape(NE, 32, 128).transpose(0, 2, 1))
    b2 = np.ascontiguousarray(np.asarray(L0_ffn_b2, dtype=np.float32))
    wel = np.ascontiguousarray(np.asarray(L0_attn_W_el, dtype=np.float32))
    bel = np.ascontiguousarray(np.asarray(L0_attn_b_el, dtype=np.float32)
                               .reshape(24, 1))
    wle = np.ascontiguousarray(np.asarray(L0_attn_W_le, dtype=np.float32))
    ble = np.ascontiguousarray(np.asarray(L0_attn_b_le, dtype=np.float32)
                               .reshape(1, D))
    wg = np.ascontiguousarray(np.asarray(L0_ffn_Wg, dtype=np.float32))
    bg = np.ascontiguousarray(np.asarray(L0_ffn_bg, dtype=np.float32)
                              .reshape(1, NE))
    x0 = np.ascontiguousarray(xf[0:1, :])

    global _NC_CACHE
    if _NC_CACHE is None:
        _NC_CACHE = build_nc()
    nc = _NC_CACHE

    common = dict(x0row=x0, w1r=w1, w2r=w2, welr=wel, belr=bel, wler=wle,
                  bler=ble, wgr=wg, bgr=bg, b1r=b1, b2r=b2,
                  ccos_hi=cos_hi, ccos_lo=cos_lo, csin_hi=sin_hi,
                  csin_lo=sin_lo, ckap=kap, ckap2=kap2, cwc2=wc2,
                  cident=identb, cht=htb, cones=onesb, cjrev=jrevb)
    in_maps = []
    for c in range(NCORE):
        m = dict(common)
        m["xs"] = np.ascontiguousarray(xf[c * TPC:(c + 1) * TPC, :])
        in_maps.append(m)

    res = run_bass_kernel_spmd(nc, in_maps, core_ids=list(range(NCORE)))
    outs = [np.asarray(r["out"], dtype=np.float32) for r in res.results]
    return np.concatenate(outs, axis=0).reshape(B, S, E)
